# version 07
# baseline (speedup 1.0000x reference)
import math
import numpy as np
from contextlib import ExitStack

import concourse.bass as bass
import concourse.mybir as mybir
from concourse.bass_utils import run_bass_kernel_spmd

F32 = mybir.dt.float32
BF16 = mybir.dt.bfloat16
AF = mybir.ActivationFunctionType
ALU = mybir.AluOpType
AX = mybir.AxisListType

MAGIC = float(3 << 22)
E = 2048
T_CORE = 1024
SC = 512
NSC = 2
NT = 4
NH = 16
NKT = 16
I_FFN = 4096
NOT = 32
RMS_EPS = 1e-5
LN_EPS = 1e-5
INV_SQ1 = 1.0 / math.sqrt(E)
INV_SQ2 = 1.0 / math.sqrt(I_FFN)



def _slopes(n=16):
    start = 2.0 ** (-2.0 ** (-(math.log2(n) - 3)))
    return np.array([start * start ** i for i in range(n)], dtype=np.float64)


def _rope_tables(positions):
    freq = (1.0 / 1024.0) ** np.linspace(0.0, 1.0, 32, dtype=np.float64)
    theta = positions[:, None].astype(np.float64) * freq[None, :]
    ntile = len(positions) // 128
    cos = np.cos(theta).reshape(ntile, 128, 32).transpose(1, 0, 2)
    sin = np.sin(theta).reshape(ntile, 128, 32).transpose(1, 0, 2)
    return (np.ascontiguousarray(cos.astype(np.float32)),
            np.ascontiguousarray(sin.astype(np.float32)))


def _hadamard(n):
    h = np.array([[1.0]])
    while h.shape[0] < n:
        h = np.block([[h, h], [h, -h]])
    return h.astype(np.float32)


def _host_bfly_perm(ntiles):
    rng = np.random.default_rng(0)
    x = rng.standard_normal((ntiles, 9))

    def fwht_ref(a):
        a = a.copy()
        h = 1
        while h < a.shape[0]:
            for i in range(0, a.shape[0], 2 * h):
                for k in range(i, i + h):
                    u, w = a[k].copy(), a[k + h].copy()
                    a[k], a[k + h] = u + w, u - w
            h *= 2
        return a

    y = fwht_ref(x)
    a = x.copy()
    s = ntiles // 2
    while s >= 1:
        for grp in range(ntiles // (2 * s)):
            base = grp * 2 * s
            for i in range(s):
                lo, hi = a[base + i].copy(), a[base + s + i].copy()
                a[base + i] = lo - hi
                a[base + s + i] = lo + hi
        s //= 2
    perm = np.zeros(ntiles, dtype=int)
    for slot in range(ntiles):
        perm[slot] = int(np.where(np.all(np.isclose(y, a[slot]), axis=1))[0][0])
    return perm


PERM16 = _host_bfly_perm(16)
PERM32 = _host_bfly_perm(32)


def _host_consts(half):
    sl = _slopes(NH)
    r = np.exp(-sl)
    c = np.arange(128, dtype=np.float64)
    diff = c[None, :] - c[:, None]
    dm = np.where(diff >= 0, r[:, None, None] ** np.where(diff >= 0, diff, 0.0)[None], 0.0)
    dmaskT = np.ascontiguousarray(dm.transpose(1, 0, 2).reshape(128, NH * 128).astype(np.float32))
    qdec = (r[:, None] ** (c[None, :] + 1.0)).reshape(1, NH * 128).astype(np.float32)
    kdec = (r[None, :] ** (127.0 - c[:, None])).astype(np.float32)
    rC = np.ascontiguousarray(np.broadcast_to((r ** 128.0)[None, :], (128, NH)).astype(np.float32))
    mask = np.full((128, 1), 1.0 if half == 1 else 0.0, dtype=np.float32)
    ropec, ropes = _rope_tables(np.arange(half * 1024, half * 1024 + 1024))
    ropecp, ropesp = _rope_tables(np.arange(0, 1024))
    return dict(dmaskT=dmaskT, qdec=np.ascontiguousarray(qdec), kdec=kdec, rCc=rC,
                maskc=mask, ropec=ropec, ropes=ropes, ropecp=ropecp, ropesp=ropesp,
                hmat=_hadamard(128), ident=np.eye(128, dtype=np.float32))


class Ctr:
    def __init__(self, handle):
        self.h = handle
        self.n = 0


def build_nc(dbg=False):
    nc = bass.Bass("TRN2", target_bir_lowering=False)

    d_xT = nc.dram_tensor("xT", [E, T_CORE], F32, kind="ExternalInput").ap()
    d_xpT = nc.dram_tensor("xpT", [E, T_CORE], F32, kind="ExternalInput").ap()
    d_xtok = nc.dram_tensor("xtok", [T_CORE, E], F32, kind="ExternalInput").ap()
    d_wq = nc.dram_tensor("wq", [E, E], F32, kind="ExternalInput").ap()
    d_wk = nc.dram_tensor("wk", [E, E], F32, kind="ExternalInput").ap()
    d_wv = nc.dram_tensor("wv", [E, E], F32, kind="ExternalInput").ap()
    d_wo = nc.dram_tensor("wo", [E, E], F32, kind="ExternalInput").ap()
    d_wupT = nc.dram_tensor("wupT", [E, I_FFN], F32, kind="ExternalInput").ap()
    d_wdnT = nc.dram_tensor("wdnT", [I_FFN, E], F32, kind="ExternalInput").ap()
    d_rmsw = nc.dram_tensor("rmsw", [128, NH], F32, kind="ExternalInput").ap()
    d_ropec = nc.dram_tensor("ropec", [128, 8, 32], F32, kind="ExternalInput").ap()
    d_ropes = nc.dram_tensor("ropes", [128, 8, 32], F32, kind="ExternalInput").ap()
    d_ropecp = nc.dram_tensor("ropecp", [128, 8, 32], F32, kind="ExternalInput").ap()
    d_ropesp = nc.dram_tensor("ropesp", [128, 8, 32], F32, kind="ExternalInput").ap()
    d_hmat = nc.dram_tensor("hmat", [128, 128], F32, kind="ExternalInput").ap()
    d_ident = nc.dram_tensor("ident", [128, 128], F32, kind="ExternalInput").ap()
    d_dmaskT = nc.dram_tensor("dmaskT", [128, NH * 128], F32, kind="ExternalInput").ap()
    d_qdec = nc.dram_tensor("qdec", [1, NH * 128], F32, kind="ExternalInput").ap()
    d_kdec = nc.dram_tensor("kdec", [128, NH], F32, kind="ExternalInput").ap()
    d_rCc = nc.dram_tensor("rCc", [128, NH], F32, kind="ExternalInput").ap()
    d_maskc = nc.dram_tensor("maskc", [128, 1], F32, kind="ExternalInput").ap()
    d_out = nc.dram_tensor("out", [T_CORE, E], F32, kind="ExternalOutput").ap()

    s_wq = nc.dram_tensor("s_wq", [E, E], BF16, kind="Internal")
    s_wk = nc.dram_tensor("s_wk", [E, E], BF16, kind="Internal")
    s_wv = nc.dram_tensor("s_wv", [E, E], BF16, kind="Internal")
    s_wo = nc.dram_tensor("s_wo", [E, E], BF16, kind="Internal")
    s_tup = nc.dram_tensor("s_tup", [E, I_FFN], BF16, kind="Internal")
    s_tdn = nc.dram_tensor("s_tdn", [I_FFN, E], BF16, kind="Internal")
    okind = "ExternalOutput" if dbg else "Internal"
    s_s = nc.dram_tensor("s_s", [T_CORE, E], F32, kind=okind)
    s_xnT = nc.dram_tensor("s_xnT", [E, T_CORE], BF16, kind=okind)
    if dbg:
        d_attnw = nc.dram_tensor("d_attnw", [E, T_CORE], BF16, kind="ExternalOutput").ap()
        d_qT = nc.dram_tensor("d_qT", [E, T_CORE], BF16, kind="ExternalOutput").ap()
        d_kT = nc.dram_tensor("d_kT", [E, T_CORE], BF16, kind="ExternalOutput").ap()
        d_v = nc.dram_tensor("d_v", [T_CORE, E], BF16, kind="ExternalOutput").ap()
        d_kv = nc.dram_tensor("d_kv", [128, E], F32, kind="ExternalOutput").ap()
        d_int1 = nc.dram_tensor("d_int1", [E, T_CORE], BF16, kind="ExternalOutput").ap()
        d_int2 = nc.dram_tensor("d_int2", [I_FFN, T_CORE], BF16, kind="ExternalOutput").ap()
        d_g = nc.dram_tensor("d_g", [T_CORE, E], F32, kind="ExternalOutput").ap()

    with ExitStack() as ctx:
        en = ctx.enter_context

        c_rope = en(nc.sbuf_tensor("c_rope", [128, 4, 8, 32], F32))
        c_hmat = en(nc.sbuf_tensor("c_hmat", [128, 128], BF16))
        c_idf = en(nc.sbuf_tensor("c_idf", [128, 128], F32))
        c_idb = en(nc.sbuf_tensor("c_idb", [128, 128], BF16))
        c_dmaskT = en(nc.sbuf_tensor("c_dmaskT", [128, NH, 128], BF16))
        c_qdec = en(nc.sbuf_tensor("c_qdec", [128, NH, 128], BF16))
        c_kdec = en(nc.sbuf_tensor("c_kdec", [128, NH], F32))
        c_rC = en(nc.sbuf_tensor("c_rC", [128, NH], F32))
        c_rmsw = en(nc.sbuf_tensor("c_rmsw", [128, NH], F32))
        c_mask = en(nc.sbuf_tensor("c_mask", [128, 1], F32))
        c_ones_c = en(nc.sbuf_tensor("c_ones_c", [128, 1], F32))
        c_ones_r = en(nc.sbuf_tensor("c_ones_r", [1, 128], F32))
        c_one11 = en(nc.sbuf_tensor("c_one11", [1, 1], F32))
        c_mg = en(nc.sbuf_tensor("c_mg", [128, 1], F32))
        c_mgn = en(nc.sbuf_tensor("c_mgn", [128, 1], F32))
        c_zero = en(nc.sbuf_tensor("c_zero", [128, 1], F32))

        kv_f = en(nc.sbuf_tensor("kv_f", [128, NH, 128], F32))
        kv_b = en(nc.sbuf_tensor("kv_b", [128, NH, 128], BF16))

        bigA = en(nc.sbuf_tensor("bigA", [128, 16384], BF16))
        bigB = en(nc.sbuf_tensor("bigB", [128, 16384], BF16))
        bigC = en(nc.sbuf_tensor("bigC", [128, 8192], BF16))
        bigD = en(nc.sbuf_tensor("bigD", [128, 8192], F32))
        xnT = en(nc.sbuf_tensor("xnT", [128, NKT, SC], BF16))
        wbuf = en(nc.sbuf_tensor("wbuf", [128, 16384], BF16))

        xtile = en(nc.sbuf_tensor("xtile", [128, 2, 512], F32))
        sqt = en(nc.sbuf_tensor("sqt", [128, 2, 512], F32))
        rowbuf = en(nc.sbuf_tensor("rowbuf", [1, 4, 512], F32))
        scosb = en(nc.sbuf_tensor("scosb", [128, 2, 128], BF16))
        qdecb = en(nc.sbuf_tensor("qdecb", [128, 2, 128], BF16))
        sqacc = en(nc.sbuf_tensor("sqacc", [128, NT, 128], F32))
        rscol = en(nc.sbuf_tensor("rscol", [128, NT], F32))
        srow = en(nc.sbuf_tensor("srow", [1, 4, 128], F32))
        acol = en(nc.sbuf_tensor("acol", [128, 16], F32))
        gcol = en(nc.sbuf_tensor("gcol", [128, NT], F32))
        wsc = en(nc.sbuf_tensor("wsc", [128, 6], F32))
        wred = en(nc.sbuf_tensor("wred", [128, 4], F32))

        psf = [en(nc.psum_tensor(f"psf{i}", [128, 512], F32)) for i in range(6)]
        psb = [en(nc.psum_tensor(f"psb{i}", [128, 512], BF16)) for i in range(2)]

        sem_t = Ctr(en(nc.semaphore("sem_t")))
        sem_v = Ctr(en(nc.semaphore("sem_v")))
        sem_s = Ctr(en(nc.semaphore("sem_s")))
        sem_g = Ctr(en(nc.semaphore("sem_g")))
        dname = ["c", "gc", "pa0", "pa1", "pq0", "pq1", "pst", "lnA", "lnB",
                 "wsA", "wsB", "xn", "sst", "up", "dn", "cx", "out", "dbg"]
        D = {n: Ctr(en(nc.semaphore(f"sem_{n}"))) for n in dname}
        ALLS = [sem_t, sem_v, sem_s, sem_g] + [D[n] for n in dname]

        def dma(eng, sem, out, in_):
            eng.dma_start(out=out, in_=in_).then_inc(sem.h, 16)
            sem.n += 16
            return sem.n

        block = en(nc.Block())

        def T(fn):
            @block.tensor
            def _(t):
                fn(t)

        def V(fn):
            @block.vector
            def _(v):
                fn(v)

        def S(fn):
            @block.scalar
            def _(s):
                fn(s)

        def G(fn):
            @block.gpsimd
            def _(g):
                fn(g)

        def Y(fn):
            @block.sync
            def _(y):
                fn(y)

        def barrier():
            snap = [(c, c.n) for c in ALLS if c.n > 0]
            for opener in (T, V, S, G, Y):
                def f(eng, snap=snap):
                    for cc, n in snap:
                        eng.wait_ge(cc.h, n)
                opener(f)

        wsl = wbuf[:].rearrange("p (a k c) -> p a k c", a=2, k=16)
        dnsl = wbuf[:].rearrange("p (k c) -> p k c", k=32)
        upsl = wbuf[:].rearrange("p (a k c) -> p a k c", a=8, k=16)
        qkst = bigD[:].bitcast(BF16).rearrange("p (a e) -> p a e", a=8)
        comb = bigD[:].rearrange("p (a f) -> p a f", a=4)
        zb = bigD[:].rearrange("p (k c) -> p k c", k=16)
        zb2 = bigD[:].rearrange("p (k c) -> p k c", k=32)
        qT = bigA[:].rearrange("p (a k c) -> p a k c", a=2, k=16)
        vB = bigB[:].rearrange("p (a m e) -> p a m e", a=2, m=4)
        attnw = bigC[:].rearrange("p (k c) -> p k c", k=16)
        xq1 = bigC[:].rearrange("p (k c) -> p k c", k=16)
        h2T = bigA[:].rearrange("p (k c) -> p k c", k=32)
        xq2 = bigB[:].rearrange("p (k c) -> p k c", k=32)
        kAp = bigA[:].rearrange("p (m e) -> p m e", m=8)
        vAp = bigB[:].rearrange("p (m e) -> p m e", m=8)
        rowv = rowbuf[:].rearrange("a r f -> a (r f)")
        sqsb = bigD[:, 0:2048].rearrange("p (h c) -> p h c", c=128)
        attf = bigD[:, 2048:4096]
        wbufF = wbuf[:].bitcast(F32)
        m1b = wbufF[:, 0:4096].rearrange("p (k c) -> p k c", c=512)
        m1f = wbufF[:, 0:4096]
        m2f = wbufF[:, 4096:8192]
        kvff = kv_f[:].rearrange("p h c -> p (h c)")
        kvbf = kv_b[:].rearrange("p h c -> p (h c)")
        rot4 = [sqt[:, 0, :], sqt[:, 1, :], xtile[:, 0, :], xtile[:, 1, :]]

        def p0(y):
            for dst, src in ((c_rope[:, 0], d_ropec), (c_rope[:, 1], d_ropes),
                             (c_rope[:, 2], d_ropecp), (c_rope[:, 3], d_ropesp)):
                dma(y, D["c"], dst, src)
            dma(y, D["c"], c_idf[:], d_ident)
            dma(y, D["c"], c_kdec[:], d_kdec)
            dma(y, D["c"], c_rC[:], d_rCc)
            dma(y, D["c"], c_rmsw[:], d_rmsw)
            dma(y, D["c"], c_mask[:], d_maskc)
        Y(p0)

        def p0g(g):
            dma(g, D["gc"], c_hmat[:], d_hmat)
            dma(g, D["gc"], c_dmaskT[:].rearrange("p h c -> p (h c)"), d_dmaskT)
            dma(g, D["gc"], c_qdec[:].rearrange("p h c -> p (h c)"), d_qdec.partition_broadcast(128))
            dma(g, D["gc"], c_idb[:], d_ident)
            g.memset(c_ones_c[:], 1.0)
            g.memset(c_ones_r[:], 1.0)
            g.memset(c_one11[:], 1.0)
            g.memset(c_mg[:], MAGIC)
            g.memset(c_mgn[:], -MAGIC)
            g.memset(c_zero[:], 0.0)
            g.memset(kvff, 0.0)
            g.memset(kvbf, 0.0)
            g.engine_nop().then_inc(sem_g.h, 1); sem_g.n += 1
            for dsrc, ddst in ((d_wq, s_wq), (d_wk, s_wk), (d_wv, s_wv), (d_wo, s_wo)):
                dma(g, D["gc"], ddst.ap(), dsrc)
        G(p0g)

        wt = bigD[:].rearrange("p (a f) -> p a f", a=2)
        NWT = NKT + NOT
        lt1 = []
        g_base = sem_g.n

        def y_p1a(y):
            for i in range(NWT):
                if i >= 2:
                    y.wait_ge(sem_g.h, g_base + i)
                sl = D["pa0"] if i % 2 == 0 else D["pa1"]
                if i < NKT:
                    dma(y, sl, wt[:, i % 2, :], d_wupT[i * 128:(i + 1) * 128, :])
                else:
                    j = i - NKT
                    dma(y, sl, wt[:, i % 2, 0:2048], d_wdnT[j * 128:(j + 1) * 128, :])
                lt1.append((sl, sl.n))
        Y(y_p1a)

        def g_p1a0(g):
            g.memset(wred[:, 0:4], 0.0)
            g.engine_nop().then_inc(sem_g.h, 1)
            sem_g.n += 1
        G(g_p1a0)
        tok_ms = sem_g.n

        def v_p1a0(v):
            v.wait_ge(sem_g.h, tok_ms)
            for i in range(NWT):
                v.wait_ge(lt1[i][0].h, lt1[i][1])
                col = 0 if i < NKT else 1
                w = wt[:, i % 2, :] if i < NKT else wt[:, i % 2, 0:2048]
                v.tensor_reduce(wred[:, 2 + col:3 + col], w, AX.X, ALU.add, apply_absolute_value=True)
                v.drain()
                v.tensor_add(wred[:, col:col + 1], wred[:, col:col + 1], wred[:, 2 + col:3 + col])
                v.drain()
                v.engine_nop().then_inc(sem_g.h, 1)
                sem_g.n += 1
        V(v_p1a0)
        tok_g = sem_g.n

        def t_p1a(t):
            t.wait_ge(sem_g.h, tok_g)
            t.matmul(psf[0][0:1, 0:2], c_ones_c[:], wred[:, 0:2], start=True, stop=True).then_inc(sem_t.h, 1)
            sem_t.n += 1
        T(t_p1a)
        tok_t1 = sem_t.n

        def v_p1a(v):
            v.wait_ge(sem_t.h, tok_t1)
            v.tensor_scalar(rowv[0:1, 0:1], psf[0][0:1, 0:1], 1.0 / (E * I_FFN), 1e-5, ALU.mult, ALU.max)
            v.tensor_scalar(rowv[0:1, 1:2], psf[0][0:1, 1:2], 1.0 / (E * I_FFN), 1e-5, ALU.mult, ALU.max)
            v.drain()
            v.reciprocal(rowv[0:1, 2:3], rowv[0:1, 0:1])
            v.reciprocal(rowv[0:1, 3:4], rowv[0:1, 1:2])
            v.drain()
            v.engine_nop().then_inc(sem_v.h, 1)
            sem_v.n += 1
        V(v_p1a)
        tok_v1 = sem_v.n

        def t_p1b(t):
            t.wait_ge(sem_v.h, tok_v1)
            t.matmul(psf[1][:, 0:4], c_ones_r[:], rowv[0:1, 0:4], start=True, stop=True).then_inc(sem_t.h, 1)
            sem_t.n += 1
        T(t_p1b)
        tok_t2 = sem_t.n

        def v_p1b(v):
            v.wait_ge(sem_t.h, tok_t2)
            v.tensor_copy(wsc[:, 0:4], psf[1][:, 0:4])
            v.drain()
            v.engine_nop().then_inc(sem_v.h, 1)
            sem_v.n += 1
        V(v_p1b)
        barrier()

        qslot = bigA[:].rearrange("p (a f) -> p a f", a=4)
        lt2 = []
        s_qb = sem_s.n
        v_qb = sem_v.n
        pst_qb = D["pst"].n

        def y_p1q(y):
            for i in range(NWT):
                if i >= 2:
                    y.wait_ge(sem_s.h, s_qb + 2 * (i - 1))
                sl = D["pq0"] if i % 2 == 0 else D["pq1"]
                if i < NKT:
                    dma(y, sl, wt[:, i % 2, :], d_wupT[i * 128:(i + 1) * 128, :])
                else:
                    j = i - NKT
                    dma(y, sl, wt[:, i % 2, 0:2048], d_wdnT[j * 128:(j + 1) * 128, :])
                lt2.append((sl, sl.n))
        Y(y_p1q)

        def s_p1q(s):
            for i in range(NWT):
                width = 4096 if i < NKT else 2048
                w = wt[:, i % 2, 0:width]
                rc = wsc[:, 2:3] if i < NKT else wsc[:, 3:4]
                tgt = bigB[:].bitcast(F32)[:, (i % 2) * 4096:(i % 2) * 4096 + width]
                s.wait_ge(lt2[i][0].h, lt2[i][1])
                if i >= 2:
                    s.wait_ge(sem_v.h, v_qb + (i - 1))
                s.activation(tgt, w, AF.Identity, bias=c_mg[:], scale=rc).then_inc(sem_s.h, 1)
                sem_s.n += 1
                s.activation(tgt, tgt, AF.Identity, bias=c_mgn[:], scale=1.0).then_inc(sem_s.h, 1)
                sem_s.n += 1
        S(s_p1q)

        def v_p1q(v):
            for i in range(NWT):
                width = 4096 if i < NKT else 2048
                tgt = bigB[:].bitcast(F32)[:, (i % 2) * 4096:(i % 2) * 4096 + width]
                v.wait_ge(sem_s.h, s_qb + 2 * (i + 1))
                if i >= 2:
                    v.wait_ge(D["pst"].h, pst_qb + 16 * (i - 1))
                v.tensor_scalar(qslot[:, i % 2, 0:width], tgt, 1.0, -1.0, ALU.min, ALU.max).then_inc(sem_v.h, 1)
                sem_v.n += 1
        V(v_p1q)

        def g_p1q(g):
            for i in range(NWT):
                width = 4096 if i < NKT else 2048
                g.wait_ge(sem_v.h, v_qb + (i + 1))
                if i < NKT:
                    dma(g, D["pst"], s_tup.ap()[i * 128:(i + 1) * 128, :], qslot[:, i % 2, 0:width])
                else:
                    j = i - NKT
                    dma(g, D["pst"], s_tdn.ap()[j * 128:(j + 1) * 128, :], qslot[:, i % 2, 0:width])
        G(g_p1q)
        barrier()

        def emit_lnb(xdram, sc0, waitextra=None):
            ltA = []
            s_base = sem_s.n
            t_base = sem_t.n

            def y_a(y):
                if waitextra is not None:
                    y.wait_ge(waitextra[0].h, waitextra[1])
                for k in range(NKT):
                    if k >= 2:
                        y.wait_ge(sem_s.h, s_base + (k - 1))
                        y.wait_ge(sem_t.h, t_base + 2 * (k - 1))
                    sl = D["lnA"] if k % 2 == 0 else D["lnB"]
                    dma(y, sl, xtile[:, k % 2, :],
                        xdram[k * 128:(k + 1) * 128, sc0 * SC:(sc0 + 1) * SC])
                    ltA.append((sl, sl.n))
            Y(y_a)

            def s_a(s):
                for k in range(NKT):
                    s.wait_ge(ltA[k][0].h, ltA[k][1])
                    if k >= 2:
                        s.wait_ge(sem_t.h, t_base + 2 * (k - 1))
                    s.activation(sqt[:, k % 2, :], xtile[:, k % 2, :], AF.Square).then_inc(sem_s.h, 1)
                    sem_s.n += 1
            S(s_a)

            def t_a(t):
                for k in range(NKT):
                    t.wait_ge(sem_s.h, s_base + k + 1)
                    t.matmul(psf[0][0:1, :], c_ones_c[:], xtile[:, k % 2, :],
                             start=(k == 0), stop=(k == NKT - 1))
                    t.matmul(psf[1][0:1, :], c_ones_c[:], sqt[:, k % 2, :],
                             start=(k == 0), stop=(k == NKT - 1)).then_inc(sem_t.h, 2)
                    sem_t.n += 2
            T(t_a)
            tok_sums = sem_t.n

            def v_a(v):
                v.wait_ge(sem_t.h, tok_sums)
                v.tensor_scalar(rowv[0:1, 0:512], psf[0][0:1, :], 1.0 / E, None, ALU.mult)
                v.tensor_scalar(rowv[0:1, 512:1024], psf[1][0:1, :], 1.0 / E, LN_EPS, ALU.mult, ALU.add)
                v.tensor_mul(rowv[0:1, 1024:1536], rowv[0:1, 0:512], rowv[0:1, 0:512])
                v.tensor_sub(rowv[0:1, 512:1024], rowv[0:1, 512:1024], rowv[0:1, 1024:1536]).then_inc(sem_v.h, 1)
                sem_v.n += 1
            V(v_a)
            tok_var = sem_v.n

            def s_b(s):
                s.wait_ge(sem_v.h, tok_var)
                s.activation(rowv[0:1, 1024:1536], rowv[0:1, 512:1024], AF.Sqrt).then_inc(sem_s.h, 1)
                sem_s.n += 1
            S(s_b)
            tok_sq = sem_s.n

            def v_b(v):
                v.wait_ge(sem_s.h, tok_sq)
                v.reciprocal(rowv[0:1, 512:1024], rowv[0:1, 1024:1536])
                v.tensor_mul(rowv[0:1, 0:512], rowv[0:1, 0:512], rowv[0:1, 512:1024]).then_inc(sem_v.h, 1)
                sem_v.n += 1
            V(v_b)
            tok_rows = sem_v.n

            def t_b(t):
                t.wait_ge(sem_v.h, tok_rows)
                t.matmul(psf[2][:], c_ones_r[:], rowv[0:1, 512:1024], start=True, stop=True)
                t.matmul(psf[3][:], c_ones_r[:], rowv[0:1, 0:512], start=True, stop=True).then_inc(sem_t.h, 1)
                sem_t.n += 1
            T(t_b)
            tok_bc = sem_t.n

            ltB = []
            v2_base = sem_v.n

            def y_b(y):
                for k in range(NKT):
                    if k < 2:
                        y.wait_ge(sem_t.h, tok_sums)
                    else:
                        y.wait_ge(sem_v.h, v2_base + 2 * (k - 1))
                    sl = D["lnA"] if k % 2 == 0 else D["lnB"]
                    dma(y, sl, xtile[:, k % 2, :],
                        xdram[k * 128:(k + 1) * 128, sc0 * SC:(sc0 + 1) * SC])
                    ltB.append((sl, sl.n))
            Y(y_b)

            def v_c(v):
                v.wait_ge(sem_t.h, tok_bc)
                for k in range(NKT):
                    v.wait_ge(ltB[k][0].h, ltB[k][1])
                    v.tensor_mul(sqt[:, k % 2, :], xtile[:, k % 2, :], psf[2][:]).then_inc(sem_v.h, 1)
                    sem_v.n += 1
                    v.tensor_sub(xnT[:, k, :], sqt[:, k % 2, :], psf[3][:]).then_inc(sem_v.h, 1)
                    sem_v.n += 1
            V(v_c)
            return (sem_v, sem_v.n)

        def load_wslice(y, sem, dst, src_ap, col0, ncol):
            view = src_ap.rearrange("(k p) c -> k p c", p=128)
            return dma(y, sem, dst, view[:, :, col0:col0 + ncol].transpose([1, 0, 2]))

        def emit_proj(ws_list, tok_xn, evac):
            bank_free = [(sem_v, 0)] * 4
            slice_user = [(sem_t, 0), (sem_t, 0)]
            si = 0
            for widx, wsrc in enumerate(ws_list):
                for n in range(4):
                    buf = si % 2
                    su = slice_user[buf]

                    wsem = D["wsA"] if buf == 0 else D["wsB"]

                    def y_l(y, buf=buf, wsrc=wsrc, n=n, su=su, wsem=wsem):
                        y.wait_ge(su[0].h, su[1])
                        load_wslice(y, wsem, wsl[:, buf], wsrc.ap(), n * 512, 512)
                    Y(y_l)
                    tok_sl = (wsem, wsem.n)
                    for m in range(NT):
                        bank = m
                        bf = bank_free[bank]

                        def t_mm(t, buf=buf, m=m, bank=bank, tok_sl=tok_sl, bf=bf, tok_xn=tok_xn):
                            t.wait_ge(tok_sl[0].h, tok_sl[1])
                            t.wait_ge(tok_xn[0].h, tok_xn[1])
                            t.wait_ge(bf[0].h, bf[1])
                            for k in range(NKT):
                                ins = t.matmul(psf[bank][:], xnT[:, k, m * 128:(m + 1) * 128],
                                               wsl[:, buf, k, :], start=(k == 0), stop=(k == NKT - 1))
                            ins.then_inc(sem_t.h, 1)
                            sem_t.n += 1
                        T(t_mm)
                        bank_free[bank] = evac(widx, n, m, psf[bank], sem_t.n)
                    slice_user[buf] = (sem_t, sem_t.n)
                    si += 1
            return bank_free

        def emit_rotary(v, tile3, m_abs, rope0):
            cosb = c_rope[:, rope0, m_abs, :].unsqueeze(1).broadcast_to([128, NH, 32])
            sinb = c_rope[:, rope0 + 1, m_abs, :].unsqueeze(1).broadcast_to([128, NH, 32])
            x1 = tile3[:, :, 0:32]
            x2 = tile3[:, :, 64:96]
            tv = [r.rearrange("p (h k) -> p h k", k=32) for r in rot4]
            v.tensor_mul(tv[0], x1, cosb)
            v.tensor_mul(tv[1], x2, sinb)
            v.tensor_mul(tv[2], x2, cosb)
            v.tensor_mul(tv[3], x1, sinb)
            v.tensor_add(x1, tv[0], tv[1])
            v.tensor_sub(x2, tv[2], tv[3]).then_inc(sem_v.h, 1)
            sem_v.n += 1
            return (sem_v, sem_v.n)

        def emit_kv_chunk(ksd_t, v_t, tok_ready, inter_tok):
            vg = []
            vg_base = sem_v.n
            for gi in range(4):
                bank = psf[4 + gi % 2]

                def t_kv(t, gi=gi, bank=bank, tr=tok_ready):
                    t.wait_ge(tr[0].h, tr[1])
                    if gi >= 2:
                        t.wait_ge(sem_v.h, vg_base + (gi - 1))
                    for hh in range(4):
                        h = gi * 4 + hh
                        ins = t.matmul(bank[:, hh * 128:(hh + 1) * 128],
                                       ksd_t[:, h * 128:(h + 1) * 128],
                                       v_t[:, h * 128:(h + 1) * 128], start=True, stop=True)
                    ins.then_inc(sem_t.h, 1)
                    sem_t.n += 1
                T(t_kv)
                tok_t = sem_t.n

                def v_kv(v, gi=gi, bank=bank, tok_t=tok_t, it=inter_tok):
                    v.wait_ge(sem_t.h, tok_t)
                    if it is not None:
                        v.wait_ge(it[0].h, it[1])
                    g0 = gi * 4
                    kvs = kv_f[:, g0:g0 + 4, :]
                    rcs = c_rC[:, g0:g0 + 4].unsqueeze(2).broadcast_to([128, 4, 128])
                    v.tensor_mul(kvs, kvs, rcs)
                    v.tensor_add(kvs, kvs, bank[:].rearrange("p (a c) -> p a c", a=4)).then_inc(sem_v.h, 1)
                    sem_v.n += 1
                V(v_kv)
                vg.append(sem_v.n)

            def g_cast(g, gt=vg[-1]):
                g.wait_ge(sem_v.h, gt)
                g.tensor_copy(kvbf, kvff).then_inc(sem_g.h, 1)
                sem_g.n += 1
            G(g_cast)
            return (sem_g, sem_g.n)

        kv_ready = (sem_g, 0)
        for sc0 in range(NSC):
            tok_xn = emit_lnb(d_xpT, sc0)
            barrier()
            ev_k = {}

            def ev_pre(widx, n, m, bank, tok_mm):
                def s_ev(s, widx=widx, n=n, m=m, bank=bank, tok_mm=tok_mm):
                    s.wait_ge(sem_t.h, tok_mm)
                    dst = kAp[:, m, n * 512:(n + 1) * 512] if widx == 0 else vAp[:, m, n * 512:(n + 1) * 512]
                    s.copy(dst, bank[:]).then_inc(sem_s.h, 1)
                    sem_s.n += 1
                S(s_ev)
                ev_k[(widx, m, n)] = sem_s.n
                return (sem_s, sem_s.n)

            emit_proj([s_wk, s_wv], tok_xn, ev_pre)
            barrier()

            for m in range(NT):
                def v_rot(v, m=m, sc0=sc0):
                    k3 = kAp[:, m, :].rearrange("p (h d) -> p h d", d=128)
                    emit_rotary(v, k3, sc0 * NT + m, 2)
                    kd = c_kdec[:].unsqueeze(2).broadcast_to([128, NH, 128])
                    v.tensor_mul(vAp[:, 4 + m, :].rearrange("p (h d) -> p h d", d=128), k3, kd).then_inc(sem_v.h, 1)
                    sem_v.n += 1
                V(v_rot)
                kv_ready = emit_kv_chunk(vAp[:, 4 + m, :], vAp[:, m, :], (sem_v, sem_v.n), None)
            barrier()

        def v_mask(v):
            v.tensor_scalar(kvff, kvff, c_mask[:], None, ALU.mult).then_inc(sem_v.h, 1)
            sem_v.n += 1
        V(v_mask)
        tok_m = sem_v.n

        def g_mask(g):
            g.wait_ge(sem_v.h, tok_m)
            g.tensor_copy(kvbf, kvff).then_inc(sem_g.h, 1)
            sem_g.n += 1
        G(g_mask)
        kv_ready = (sem_g, sem_g.n)
        if dbg:
            def y_dkv(y, kvt=kv_ready):
                y.wait_ge(kvt[0].h, kvt[1])
                dma(y, D["dbg"], d_kv, kvff)
            Y(y_dkv)
        barrier()

        for sc0 in range(NSC):
            tok_xn = emit_lnb(d_xT, sc0)
            barrier()

            def y_xnst(y, sc0=sc0):
                view = s_xnT.ap().rearrange("(k p) t -> k p t", p=128)
                dma(y, D["xn"], view[:, :, sc0 * SC:(sc0 + 1) * SC].transpose([1, 0, 2]), xnT[:])
            Y(y_xnst)

            rot_done = {}
            tp_done = {}

            def ev_main(widx, n, m, bank, tok_mm, sc0=sc0):
                if widx == 2:
                    def s_ev(s, n=n, m=m, bank=bank, tok_mm=tok_mm):
                        s.wait_ge(sem_t.h, tok_mm)
                        s.copy(vB[:, 0, m, n * 512:(n + 1) * 512], bank[:]).then_inc(sem_s.h, 1)
                        sem_s.n += 1
                    S(s_ev)
                    return (sem_s, sem_s.n)
                slot = widx * 4 + m

                def s_ev(s, slot=slot, n=n, bank=bank, tok_mm=tok_mm):
                    s.wait_ge(sem_t.h, tok_mm)
                    s.copy(qkst[:, slot, n * 512:(n + 1) * 512], bank[:]).then_inc(sem_s.h, 1)
                    sem_s.n += 1
                S(s_ev)
                ev = sem_s.n
                if n == 3:
                    def v_rot(v, widx=widx, m=m, slot=slot, ev=ev, sc0=sc0):
                        v.wait_ge(sem_s.h, ev)
                        t3 = qkst[:, slot, :].rearrange("p (h d) -> p h d", d=128)
                        emit_rotary(v, t3, sc0 * NT + m, 0)
                        if widx == 1:
                            kd = c_kdec[:].unsqueeze(2).broadcast_to([128, NH, 128])
                            v.tensor_mul(vB[:, 1, m, :].rearrange("p (h d) -> p h d", d=128),
                                         t3, kd).then_inc(sem_v.h, 1)
                            sem_v.n += 1
                    V(v_rot)
                    rot_done[(widx, m)] = sem_v.n

                    s_base = sem_s.n
                    prev_tp = tp_done.get("last", 0)

                    def t_tp(t, slot=slot, rd=sem_v.n, s_base=s_base, prev_tp=prev_tp):
                        t.wait_ge(sem_v.h, rd)
                        for tau in range(NKT):
                            if tau < 2:
                                if prev_tp:
                                    t.wait_ge(sem_s.h, prev_tp)
                            else:
                                t.wait_ge(sem_s.h, s_base + tau - 1)
                            ins = t.transpose(psb[tau % 2][:, 0:128],
                                              qkst[:, slot, tau * 128:(tau + 1) * 128], c_idb[:])
                            ins.then_inc(sem_t.h, 1)
                            sem_t.n += 1
                    T(t_tp)
                    t_first = sem_t.n - NKT

                    def s_tp(s, widx=widx, m=m, t_first=t_first):
                        for tau in range(NKT):
                            s.wait_ge(sem_t.h, t_first + tau + 1)
                            s.copy(qT[:, widx, tau, m * 128:(m + 1) * 128],
                                   psb[tau % 2][:, 0:128]).then_inc(sem_s.h, 1)
                            sem_s.n += 1
                    S(s_tp)
                    tp_done[(widx, m)] = sem_s.n
                    tp_done["last"] = sem_s.n
                return (sem_s, ev)

            emit_proj([s_wq, s_wk, s_wv], tok_xn, ev_main)
            barrier()

            for j in range(NT):
                sc_ev = {}
                at_ev = {}
                at_tok = {}
                for h in range(NH):
                    sbank = psf[h % 2]
                    abank = psf[2 + h % 2]

                    def t_sc(t, h=h, j=j, sbank=sbank):
                        if (h - 2) in sc_ev:
                            t.wait_ge(sem_v.h, sc_ev[h - 2])
                        ins = t.matmul(sbank[:, 0:128], qT[:, 1, h, j * 128:(j + 1) * 128],
                                       qT[:, 0, h, j * 128:(j + 1) * 128], start=True, stop=True)
                        ins.then_inc(sem_t.h, 1)
                        sem_t.n += 1
                    T(t_sc)
                    tok_sc = sem_t.n

                    def v_sc(v, h=h, j=j, sbank=sbank, tok_sc=tok_sc):
                        v.wait_ge(sem_t.h, tok_sc)
                        if (h - 2) in at_tok:
                            v.wait_ge(sem_t.h, at_tok[h - 2])
                        v.tensor_mul(scosb[:, h % 2, :], sbank[:, 0:128], c_dmaskT[:, h, :]).then_inc(sem_v.h, 1)
                        sem_v.n += 1
                        v.tensor_mul(qdecb[:, h % 2, :], qT[:, 0, h, j * 128:(j + 1) * 128],
                                     c_qdec[:, h, :]).then_inc(sem_v.h, 1)
                        sem_v.n += 1
                    V(v_sc)
                    sc_ev[h] = sem_v.n - 1
                    tok_vsc = sem_v.n

                    def t_at(t, h=h, j=j, abank=abank, tok_vsc=tok_vsc, kvt=kv_ready):
                        t.wait_ge(sem_v.h, tok_vsc)
                        t.wait_ge(kvt[0].h, kvt[1])
                        if (h - 2) in at_ev:
                            t.wait_ge(sem_s.h, at_ev[h - 2])
                        t.matmul(abank[:, 0:128], vB[:, 0, j, h * 128:(h + 1) * 128],
                                 scosb[:, h % 2, :], start=True, stop=False)
                        ins = t.matmul(abank[:, 0:128], kv_b[:, h, :], qdecb[:, h % 2, :],
                                       start=False, stop=True)
                        ins.then_inc(sem_t.h, 1)
                        sem_t.n += 1
                    T(t_at)
                    at_tok[h] = sem_t.n
                    tok_at = sem_t.n

                    def s_at(s, h=h, j=j, abank=abank, tok_at=tok_at):
                        s.wait_ge(sem_t.h, tok_at)
                        s.activation(sqsb[:, h, :], abank[:, 0:128], AF.Square).then_inc(sem_s.h, 1)
                        sem_s.n += 1
                        s.activation(attnw[:, h, j * 128:(j + 1) * 128], abank[:, 0:128], AF.Identity,
                                     bias=c_zero[:], scale=c_rmsw[:, h:h + 1]).then_inc(sem_s.h, 1)
                        sem_s.n += 1
                    S(s_at)
                    at_ev[h] = sem_s.n

                def v_tree(v, j=j, st=sem_s.n):
                    v.wait_ge(sem_s.h, st)
                    sq2 = bigD[:, 0:2048]
                    v.tensor_add(attf[:, 0:1024], sq2[:, 0:1024], sq2[:, 1024:2048])
                    v.tensor_add(attf[:, 1024:1536], attf[:, 0:512], attf[:, 512:1024])
                    v.tensor_add(attf[:, 1536:1792], attf[:, 1024:1280], attf[:, 1280:1536])
                    v.tensor_add(sqacc[:, j, :], attf[:, 1536:1664], attf[:, 1664:1792]).then_inc(sem_v.h, 1)
                    sem_v.n += 1
                V(v_tree)

                last = (sc0 == NSC - 1 and j == NT - 1)
                if not last:
                    kv_ready = emit_kv_chunk(vB[:, 1, j, :], vB[:, 0, j, :],
                                             (sem_v, sem_v.n), (sem_t, sem_t.n))
            barrier()

            for j in range(NT):
                def t_vr(t, j=j):
                    t.matmul(psf[0][0:1, 0:128], c_ones_c[:], sqacc[:, j, :], start=True, stop=True).then_inc(sem_t.h, 1)
                    sem_t.n += 1
                T(t_vr)
                tk1 = sem_t.n

                def v_vr(v, tk1=tk1):
                    v.wait_ge(sem_t.h, tk1)
                    v.tensor_scalar(srow[0:1, 0, :], psf[0][0:1, 0:128], 1.0 / E, RMS_EPS,
                                    ALU.mult, ALU.add).then_inc(sem_v.h, 1)
                    sem_v.n += 1
                V(v_vr)
                tk2 = sem_v.n

                def s_sx(s, tk2=tk2):
                    s.wait_ge(sem_v.h, tk2)
                    s.activation(srow[0:1, 1, :], srow[0:1, 0, :], AF.Sqrt).then_inc(sem_s.h, 1)
                    sem_s.n += 1
                S(s_sx)
                tk3 = sem_s.n

                def v_rc(v, tk3=tk3):
                    v.wait_ge(sem_s.h, tk3)
                    v.reciprocal(srow[0:1, 2, :], srow[0:1, 1, :]).then_inc(sem_v.h, 1)
                    sem_v.n += 1
                V(v_rc)
                tk4 = sem_v.n

                def t_tr(t, tk4=tk4):
                    t.wait_ge(sem_v.h, tk4)
                    t.matmul(psf[1][:, 0:1], srow[0:1, 2, :], c_one11[:], start=True, stop=True).then_inc(sem_t.h, 1)
                    sem_t.n += 1
                T(t_tr)
                tk5 = sem_t.n

                def v_cp(v, j=j, tk5=tk5):
                    v.wait_ge(sem_t.h, tk5)
                    v.tensor_copy(rscol[:, j:j + 1], psf[1][:, 0:1])
                    v.drain()
                    v.engine_nop().then_inc(sem_v.h, 1)
                    sem_v.n += 1
                V(v_cp)
            barrier()

            wo_bank_free = {2: (sem_v, 0), 3: (sem_v, 0)}
            wo_slice_user = [(sem_t, 0), (sem_t, 0)]
            for n in range(4):
                buf = n % 2
                su = wo_slice_user[buf]

                wsem = D["wsA"] if buf == 0 else D["wsB"]

                def y_wo(y, buf=buf, n=n, su=su, wsem=wsem):
                    y.wait_ge(su[0].h, su[1])
                    load_wslice(y, wsem, wsl[:, buf], s_wo.ap(), n * 512, 512)
                Y(y_wo)
                tok_sl = (wsem, wsem.n)
                for m in range(NT):
                    bank = 2 + m % 2
                    bf = wo_bank_free[bank]

                    def t_wo(t, buf=buf, m=m, bank=bank, tok_sl=tok_sl, bf=bf):
                        t.wait_ge(tok_sl[0].h, tok_sl[1])
                        t.wait_ge(bf[0].h, bf[1])
                        for k in range(NH):
                            ins = t.matmul(psf[bank][:], attnw[:, k, m * 128:(m + 1) * 128],
                                           wsl[:, buf, k, :], start=(k == 0), stop=(k == NH - 1))
                        ins.then_inc(sem_t.h, 1)
                        sem_t.n += 1
                    T(t_wo)
                    tokmm = sem_t.n

                    def v_ev(v, n=n, m=m, bank=bank, tokmm=tokmm):
                        v.wait_ge(sem_t.h, tokmm)
                        v.tensor_scalar(comb[:, m, n * 512:(n + 1) * 512], psf[bank][:],
                                        rscol[:, m:m + 1], None, ALU.mult).then_inc(sem_v.h, 1)
                        sem_v.n += 1
                    V(v_ev)
                    wo_bank_free[bank] = (sem_v, sem_v.n)
                    if n == 3:
                        def y_st(y, m=m, sc0=sc0, tokev=sem_v.n):
                            y.wait_ge(sem_v.h, tokev)
                            gm = sc0 * NT + m
                            dma(y, D["sst"], s_s.ap()[gm * 128:(gm + 1) * 128, :], comb[:, m, :])
                        Y(y_st)
                wo_slice_user[buf] = (sem_t, sem_t.n)
            if dbg:
                def y_dbg1(y, sc0=sc0):
                    qv = d_qT.rearrange("(k p) t -> k p t", p=128)
                    kvv = d_kT.rearrange("(k p) t -> k p t", p=128)
                    av = d_attnw.rearrange("(k p) t -> k p t", p=128)
                    dma(y, D["dbg"], qv[:, :, sc0 * SC:(sc0 + 1) * SC].transpose([1, 0, 2]), qT[:, 0])
                    dma(y, D["dbg"], kvv[:, :, sc0 * SC:(sc0 + 1) * SC].transpose([1, 0, 2]), qT[:, 1])
                    dma(y, D["dbg"], av[:, :, sc0 * SC:(sc0 + 1) * SC].transpose([1, 0, 2]), attnw[:])
                    for m in range(NT):
                        gm = sc0 * NT + m
                        dma(y, D["dbg"], d_v[gm * 128:(gm + 1) * 128, :], vB[:, 0, m, :])
                Y(y_dbg1)
            barrier()

        for sc0 in range(NSC):
            def y_xn(y, sc0=sc0):
                view = s_xnT.ap().rearrange("(k p) t -> k p t", p=128)
                dma(y, D["xn"], xnT[:], view[:, :, sc0 * SC:(sc0 + 1) * SC].transpose([1, 0, 2]))
            Y(y_xn)
            tok_xn = (D["xn"], D["xn"].n)
            barrier()

            ev_base = sem_v.n
            for tau in range(NKT):
                bank = psf[tau % 2]

                def t_f1(t, tau=tau, bank=bank, tok_xn=tok_xn, ev_base=ev_base):
                    t.wait_ge(tok_xn[0].h, tok_xn[1])
                    if tau >= 2:
                        t.wait_ge(sem_v.h, ev_base + (tau - 1))
                    t.matmul(bank[:], c_hmat[:], xnT[:, tau, :], start=True, stop=True).then_inc(sem_t.h, 1)
                    sem_t.n += 1
                T(t_f1)
                tk = sem_t.n

                def v_f1(v, tau=tau, bank=bank, tk=tk):
                    v.wait_ge(sem_t.h, tk)
                    v.tensor_copy(zb[:, tau, :], bank[:]).then_inc(sem_v.h, 1)
                    sem_v.n += 1
                V(v_f1)
            tok = sem_v.n
            stride = NKT // 2
            while stride >= 1:
                def v_bf(v, stride=stride, tok=tok):
                    v.wait_ge(sem_v.h, tok)
                    ins = None
                    for grp in range(NKT // (2 * stride)):
                        base = grp * 2 * stride
                        lo = zb[:, base:base + stride, :].rearrange("p k c -> p (k c)")
                        hi = zb[:, base + stride:base + 2 * stride, :].rearrange("p k c -> p (k c)")
                        v.tensor_sub(lo, lo, hi)
                        ins = v.scalar_tensor_tensor(hi, hi, 2.0, lo, ALU.mult, ALU.add)
                    ins.then_inc(sem_v.h, 1)
                    sem_v.n += 1
                V(v_bf)
                tok = sem_v.n
                stride //= 2

            def v_q1t(v, tok=tok):
                v.wait_ge(sem_v.h, tok)
                cur = NKT
                first = True
                while cur > 1:
                    nh = cur // 2
                    w = nh * SC
                    if first:
                        a = zb[:, 0:nh, :].rearrange("p k c -> p (k c)")
                        b = zb[:, nh:cur, :].rearrange("p k c -> p (k c)")
                    else:
                        a, b = m1f[:, 0:w], m1f[:, w:2 * w]
                        a2, b2 = m2f[:, 0:w], m2f[:, w:2 * w]
                    v.tensor_tensor(m1f[:, 0:w], a, b, ALU.max)
                    if first:
                        v.tensor_tensor(m2f[:, 0:w], a, b, ALU.min)
                    else:
                        v.tensor_tensor(m2f[:, 0:w], a2, b2, ALU.min)
                    first = False
                    cur = nh
                v.engine_nop().then_inc(sem_v.h, 1)
                sem_v.n += 1
            V(v_q1t)
            tok_tree = sem_v.n

            for mt in range(NT):
                def t_m1(t, mt=mt, tok_tree=tok_tree):
                    t.wait_ge(sem_v.h, tok_tree)
                    if mt > 0:
                        t.wait_ge(sem_v.h, tok_tree + mt)
                    t.transpose(psf[4][:, 0:128], m1f[:, mt * 128:(mt + 1) * 128].unsqueeze(1).squeeze(1), c_idf[:])
                    t.transpose(psf[4][:, 128:256], m2f[:, mt * 128:(mt + 1) * 128].unsqueeze(1).squeeze(1),
                                c_idf[:]).then_inc(sem_t.h, 1)
                    sem_t.n += 1
                T(t_m1)
                tk = sem_t.n

                def v_m1(v, mt=mt, tk=tk):
                    v.wait_ge(sem_t.h, tk)
                    v.reduce_max(acol[:, mt:mt + 1], psf[4][:, 0:128], axis=AX.X)
                    v.tensor_reduce(acol[:, 14:15], psf[4][:, 128:256], AX.X, ALU.min)
                    v.drain()
                    v.tensor_scalar(acol[:, 14:15], acol[:, 14:15], -1.0, None, ALU.mult)
                    v.drain()
                    v.tensor_tensor(acol[:, mt:mt + 1], acol[:, mt:mt + 1], acol[:, 14:15], ALU.max)
                    v.drain()
                    v.engine_nop().then_inc(sem_v.h, 1)
                    sem_v.n += 1
                V(v_m1)
            tok_cols = sem_v.n

            def v_a1(v, tok_cols=tok_cols):
                v.wait_ge(sem_v.h, tok_cols)
                v.tensor_scalar(acol[:, 0:4], acol[:, 0:4], INV_SQ1, 1e-5, ALU.mult, ALU.max)
                v.drain()
                for mt in range(NT):
                    v.reciprocal(acol[:, 4 + mt:5 + mt], acol[:, mt:mt + 1])
                v.drain()
                v.tensor_scalar(acol[:, 4:8], acol[:, 4:8], 127.0 * INV_SQ1, None, ALU.mult)
                v.drain()
                v.engine_nop().then_inc(sem_v.h, 1)
                sem_v.n += 1
            V(v_a1)
            tok_a1 = sem_v.n

            def t_sr(t, tok_a1=tok_a1):
                t.wait_ge(sem_v.h, tok_a1)
                ins = None
                for mt in range(NT):
                    ins = t.matmul(psf[5][0:1, mt * 128:(mt + 1) * 128], acol[:, 4 + mt:5 + mt],
                                   c_idf[:], start=True, stop=True)
                ins.then_inc(sem_t.h, 1)
                sem_t.n += 1
            T(t_sr)
            tk_sr = sem_t.n

            def v_sr(v, tk_sr=tk_sr):
                v.wait_ge(sem_t.h, tk_sr)
                v.tensor_copy(rowv[0:1, 1536:2048], psf[5][0:1, :]).then_inc(sem_v.h, 1)
                sem_v.n += 1
            V(v_sr)
            tk_sr2 = sem_v.n

            def t_sb(t, tk_sr2=tk_sr2):
                t.wait_ge(sem_v.h, tk_sr2)
                t.matmul(psf[2][:], c_ones_r[:], rowv[0:1, 1536:2048], start=True, stop=True).then_inc(sem_t.h, 1)
                sem_t.n += 1
            T(t_sb)
            tk_sb = sem_t.n

            rq_base = sem_v.n
            for tau in range(NKT):
                def v_rq(v, tau=tau, tk_sb=tk_sb):
                    v.wait_ge(sem_t.h, tk_sb)
                    v.tensor_mul(sqt[:, 0, :], zb[:, tau, :], psf[2][:])
                    v.tensor_scalar(sqt[:, 0, :], sqt[:, 0, :], MAGIC, MAGIC, ALU.add, ALU.subtract)
                    v.tensor_scalar(xq1[:, tau, :], sqt[:, 0, :], 127.0, -127.0, ALU.min, ALU.max).then_inc(sem_v.h, 1)
                    sem_v.n += 1
                V(v_rq)
            tok_xq1 = sem_v.n
            if dbg:
                def y_dq1(y, sc0=sc0, tok_xq1=tok_xq1):
                    y.wait_ge(sem_v.h, tok_xq1)
                    dv = d_int1.rearrange("(k p) t -> k p t", p=128)
                    dma(y, D["dbg"], dv[:, :, sc0 * SC:(sc0 + 1) * SC].transpose([1, 0, 2]), xq1[:])
                Y(y_dq1)
            barrier()

            up_view = s_tup.ap().rearrange("(k p) o -> k p o", p=128)
            up_ev = {}
            up_tok = {}
            for og in range(4):
                def y_up(y, og=og, tok_xq1=tok_xq1):
                    if og == 0:
                        y.wait_ge(sem_v.h, tok_xq1)
                        y.wait_ge(D["pst"].h, D["pst"].n)
                    if og >= 1:
                        y.wait_ge(sem_t.h, up_tok[og - 1])
                    for slot in range(8):
                        om = og * 8 + slot
                        dma(y, D["up"], upsl[:, slot],
                            up_view[:, :, om * 128:(om + 1) * 128].transpose([1, 0, 2]))
                Y(y_up)
                tok_up = D["up"].n
                for slot in range(8):
                    om = og * 8 + slot
                    bank = psf[om % 4]

                    def t_up(t, slot=slot, om=om, bank=bank, tok_up=tok_up, tok_xq1=tok_xq1):
                        t.wait_ge(D["up"].h, tok_up)
                        t.wait_ge(sem_v.h, tok_xq1)
                        if om >= 4:
                            t.wait_ge(sem_s.h, up_ev[om - 4])
                        for k in range(NKT):
                            log = int(PERM16[k])
                            ins = t.matmul(bank[:], upsl[:, slot, log, :], xq1[:, k, :],
                                           start=(k == 0), stop=(k == NKT - 1))
                        ins.then_inc(sem_t.h, 1)
                        sem_t.n += 1
                    T(t_up)
                    tk = sem_t.n

                    def s_up(s, om=om, bank=bank, tk=tk):
                        s.wait_ge(sem_t.h, tk)
                        s.activation(sqt[:, om % 2, :], bank[:], AF.Relu).then_inc(sem_s.h, 1)
                        sem_s.n += 1
                        s.activation(h2T[:, om, :], sqt[:, om % 2, :], AF.Square).then_inc(sem_s.h, 1)
                        sem_s.n += 1
                    S(s_up)
                    up_ev[om] = sem_s.n
                up_tok[og] = sem_t.n
            barrier()

            for slab in range(2):
                c0 = slab * 256
                ev2 = sem_v.n
                for tau in range(NOT):
                    bank = psf[tau % 2]

                    def t_f2(t, tau=tau, bank=bank, ev2=ev2, c0=c0):
                        if tau >= 2:
                            t.wait_ge(sem_v.h, ev2 + (tau - 1))
                        t.matmul(bank[:, 0:256], c_hmat[:], h2T[:, tau, c0:c0 + 256],
                                 start=True, stop=True).then_inc(sem_t.h, 1)
                        sem_t.n += 1
                    T(t_f2)
                    tk = sem_t.n

                    def v_f2(v, tau=tau, bank=bank, tk=tk):
                        v.wait_ge(sem_t.h, tk)
                        v.tensor_copy(zb2[:, tau, 0:256], bank[:, 0:256]).then_inc(sem_v.h, 1)
                        sem_v.n += 1
                    V(v_f2)
                tok = sem_v.n
                stride = NOT // 2
                while stride >= 1:
                    def v_b2(v, stride=stride, tok=tok):
                        v.wait_ge(sem_v.h, tok)
                        ins = None
                        for grp in range(NOT // (2 * stride)):
                            base = grp * 2 * stride
                            lo = zb2[:, base:base + stride, 0:256].rearrange("p k c -> p (k c)")
                            hi = zb2[:, base + stride:base + 2 * stride, 0:256].rearrange("p k c -> p (k c)")
                            v.tensor_sub(lo, lo, hi)
                            ins = v.scalar_tensor_tensor(hi, hi, 2.0, lo, ALU.mult, ALU.add)
                        ins.then_inc(sem_v.h, 1)
                        sem_v.n += 1
                    V(v_b2)
                    tok = sem_v.n
                    stride //= 2

                m1v = m1f.rearrange("p (k c) -> p k c", c=256)

                def v_q2t(v, tok=tok):
                    v.wait_ge(sem_v.h, tok)
                    cur = NOT
                    first = True
                    while cur > 1:
                        nh = cur // 2
                        w = nh * 256
                        if first:
                            a = zb2[:, 0:nh, 0:256].rearrange("p k c -> p (k c)")
                            b = zb2[:, nh:cur, 0:256].rearrange("p k c -> p (k c)")
                            v.tensor_tensor(m1f[:, 0:w], a, b, ALU.max)
                            v.tensor_tensor(m2f[:, 0:w], a, b, ALU.min)
                        else:
                            v.tensor_tensor(m1f[:, 0:w], m1f[:, 0:w], m1f[:, w:2 * w], ALU.max)
                            v.tensor_tensor(m2f[:, 0:w], m2f[:, 0:w], m2f[:, w:2 * w], ALU.min)
                        first = False
                        cur = nh
                    v.engine_nop().then_inc(sem_v.h, 1)
                    sem_v.n += 1
                V(v_q2t)
                tok_tr2 = sem_v.n

                prev_vm2 = [0]
                for mt in range(2):
                    gm = slab * 2 + mt

                    def t_m2(t, mt=mt, tok_tr2=tok_tr2, pv=prev_vm2[0]):
                        t.wait_ge(sem_v.h, tok_tr2)
                        if pv:
                            t.wait_ge(sem_v.h, pv)
                        t.transpose(psf[4][:, 0:128], m1f[:, mt * 128:(mt + 1) * 128].unsqueeze(1).squeeze(1),
                                    c_idf[:])
                        t.transpose(psf[4][:, 128:256], m2f[:, mt * 128:(mt + 1) * 128].unsqueeze(1).squeeze(1),
                                    c_idf[:]).then_inc(sem_t.h, 1)
                        sem_t.n += 1
                    T(t_m2)
                    tk = sem_t.n

                    def v_m2(v, tk=tk):
                        v.wait_ge(sem_t.h, tk)
                        v.reduce_max(acol[:, 8:9], psf[4][:, 0:128], axis=AX.X)
                        v.tensor_reduce(acol[:, 14:15], psf[4][:, 128:256], AX.X, ALU.min)
                        v.tensor_scalar(acol[:, 14:15], acol[:, 14:15], -1.0, None, ALU.mult)
                        v.tensor_tensor(acol[:, 8:9], acol[:, 8:9], acol[:, 14:15],
                                        ALU.max).then_inc(sem_v.h, 1)
                        sem_v.n += 1
                    V(v_m2)
                    prev_vm2[0] = sem_v.n
                    tok_c2 = sem_v.n

                    def v_g2(v, gm=gm, tok_c2=tok_c2):
                        v.wait_ge(sem_v.h, tok_c2)
                        v.tensor_mul(acol[:, 10:11], acol[:, 0 + gm:1 + gm], wsc[:, 0:1])
                        v.drain()
                        v.tensor_scalar(acol[:, 10:11], acol[:, 10:11], 1.0 / 127.0, None, ALU.mult)
                        v.drain()
                        v.tensor_mul(acol[:, 10:11], acol[:, 10:11], acol[:, 10:11])
                        v.drain()
                        v.reciprocal(acol[:, 11:12], acol[:, 10:11])
                        v.drain()
                        v.tensor_scalar(acol[:, 11:12], acol[:, 11:12], 1e-5, None, ALU.mult)
                        v.tensor_scalar(acol[:, 8:9], acol[:, 8:9], INV_SQ2, None, ALU.mult)
                        v.drain()
                        v.tensor_tensor(acol[:, 8:9], acol[:, 8:9], acol[:, 11:12], ALU.max)
                        v.drain()
                        v.reciprocal(acol[:, 12:13], acol[:, 8:9])
                        v.drain()
                        v.tensor_scalar(acol[:, 12:13], acol[:, 12:13], 127.0 * INV_SQ2, None, ALU.mult)
                        v.tensor_mul(acol[:, 13:14], acol[:, 10:11], acol[:, 8:9])
                        v.drain()
                        v.tensor_mul(acol[:, 13:14], acol[:, 13:14], wsc[:, 1:2])
                        v.drain()
                        v.tensor_scalar(gcol[:, gm:gm + 1], acol[:, 13:14], 1.0 / 127.0, None, ALU.mult)
                        v.drain()
                        v.engine_nop().then_inc(sem_v.h, 1)
                        sem_v.n += 1
                    V(v_g2)
                    tok_g2 = sem_v.n

                    def t_r2(t, mt=mt, tok_g2=tok_g2):
                        t.wait_ge(sem_v.h, tok_g2)
                        t.matmul(psf[5][0:1, mt * 128:(mt + 1) * 128], acol[:, 12:13], c_idf[:],
                                 start=True, stop=True).then_inc(sem_t.h, 1)
                        sem_t.n += 1
                    T(t_r2)
                    tk2 = sem_t.n

                    def v_r2(v, mt=mt, tk2=tk2):
                        v.wait_ge(sem_t.h, tk2)
                        v.tensor_copy(srow[0:1, 2 + mt, :], psf[5][0:1, mt * 128:(mt + 1) * 128]).then_inc(sem_v.h, 1)
                        sem_v.n += 1
                    V(v_r2)
                tok_rows2 = sem_v.n

                def t_b2(t, tok_rows2=tok_rows2):
                    t.wait_ge(sem_v.h, tok_rows2)
                    t.matmul(psf[3][:, 0:256], c_ones_r[:],
                             srow[0:1, 2:4, :].rearrange("a r c -> a (r c)"),
                             start=True, stop=True).then_inc(sem_t.h, 1)
                    sem_t.n += 1
                T(t_b2)
                tk_b2 = sem_t.n

                for tau in range(NOT):
                    def v_rq2(v, tau=tau, c0=c0, tk_b2=tk_b2):
                        v.wait_ge(sem_t.h, tk_b2)
                        seg = zb2[:, tau, 0:256]
                        v.tensor_mul(sqt[:, 0, 0:256], seg, psf[3][:, 0:256])
                        v.tensor_scalar(sqt[:, 0, 0:256], sqt[:, 0, 0:256], MAGIC, MAGIC, ALU.add, ALU.subtract)
                        v.tensor_scalar(xq2[:, tau, c0:c0 + 256], sqt[:, 0, 0:256],
                                        127.0, -127.0, ALU.min, ALU.max).then_inc(sem_v.h, 1)
                        sem_v.n += 1
                    V(v_rq2)
            tok_xq2 = sem_v.n
            if dbg:
                def y_dq2(y, sc0=sc0, tok_xq2=tok_xq2):
                    y.wait_ge(sem_v.h, tok_xq2)
                    dv = d_int2.rearrange("(k p) t -> k p t", p=128)
                    dma(y, D["dbg"], dv[:, :, sc0 * SC:(sc0 + 1) * SC].transpose([1, 0, 2]), xq2[:])
                Y(y_dq2)
            barrier()

            dn_view = s_tdn.ap().rearrange("(k p) o -> k p o", p=128)
            dn_bank_free = {0: (sem_s, 0), 1: (sem_s, 0)}
            dn_tok = {}
            for n in range(4):
                def y_dn(y, n=n, tok_xq2=tok_xq2):
                    if n == 0:
                        y.wait_ge(sem_v.h, tok_xq2)
                        y.wait_ge(D["pst"].h, D["pst"].n)
                    if n >= 1:
                        y.wait_ge(sem_t.h, dn_tok[n - 1])
                    dma(y, D["dn"], dnsl[:], dn_view[:, :, n * 512:(n + 1) * 512].transpose([1, 0, 2]))
                Y(y_dn)
                tok_dn = D["dn"].n
                for m in range(NT):
                    bank = m % 2
                    bf = dn_bank_free[bank]

                    def t_dn(t, n=n, m=m, bank=bank, tok_dn=tok_dn, bf=bf, tok_xq2=tok_xq2):
                        t.wait_ge(D["dn"].h, tok_dn)
                        t.wait_ge(sem_v.h, tok_xq2)
                        t.wait_ge(bf[0].h, bf[1])
                        for k in range(NOT):
                            log = int(PERM32[k])
                            ins = t.matmul(psf[bank][:], xq2[:, k, m * 128:(m + 1) * 128],
                                           dnsl[:, log, :], start=(k == 0), stop=(k == NOT - 1))
                        ins.then_inc(sem_t.h, 1)
                        sem_t.n += 1
                    T(t_dn)
                    tk = sem_t.n

                    def s_dn(s, n=n, m=m, bank=bank, tk=tk):
                        s.wait_ge(sem_t.h, tk)
                        s.activation(comb[:, m, n * 512:(n + 1) * 512], psf[bank][:], AF.Identity,
                                     bias=c_zero[:], scale=gcol[:, m:m + 1]).then_inc(sem_s.h, 1)
                        sem_s.n += 1
                    S(s_dn)
                    dn_bank_free[bank] = (sem_s, sem_s.n)
                dn_tok[n] = sem_t.n

            cb_tok = {}
            for m in range(NT):
                gm = sc0 * NT + m

                def y_cx(y, gm=gm, m=m, dtk=None):
                    y.wait_ge(sem_t.h, dn_tok[3])
                    if (m - 1) in cb_tok:
                        y.wait_ge(sem_v.h, cb_tok[m - 1])
                    dma(y, D["cx"], m1b[:, 0:4, :].rearrange("p k c -> p (k c)"),
                        d_xtok[gm * 128:(gm + 1) * 128, :])
                    dma(y, D["cx"], m1b[:, 4:8, :].rearrange("p k c -> p (k c)"),
                        s_s.ap()[gm * 128:(gm + 1) * 128, :])
                Y(y_cx)
                tok_cx = D["cx"].n

                def s_si(s, m=m):
                    s.activation(comb[:, m, :], comb[:, m, :], AF.Silu).then_inc(sem_s.h, 1)
                    sem_s.n += 1
                S(s_si)
                tok_si = sem_s.n
                if dbg:
                    def y_dg(y, gm=gm, m=m, tok_si=tok_si):
                        y.wait_ge(sem_s.h, tok_si)
                        dma(y, D["dbg"], d_g[gm * 128:(gm + 1) * 128, :], comb[:, m, :])
                    Y(y_dg)
                    dbg_g_tok = D["dbg"].n

                def v_cb(v, m=m, tok_cx=tok_cx, tok_si=tok_si):
                    if dbg:
                        v.wait_ge(D["dbg"].h, dbg_g_tok)
                    v.wait_ge(D["cx"].h, tok_cx)
                    v.wait_ge(sem_s.h, tok_si)
                    sslice = m1b[:, 4:8, :].rearrange("p k c -> p (k c)")
                    xslice = m1b[:, 0:4, :].rearrange("p k c -> p (k c)")
                    v.tensor_mul(comb[:, m, :], comb[:, m, :], sslice)
                    v.tensor_add(comb[:, m, :], comb[:, m, :], xslice).then_inc(sem_v.h, 1)
                    sem_v.n += 1
                V(v_cb)
                tok_cb = sem_v.n
                cb_tok[m] = tok_cb

                def y_out(y, gm=gm, m=m, tok_cb=tok_cb):
                    y.wait_ge(sem_v.h, tok_cb)
                    dma(y, D["out"], d_out[gm * 128:(gm + 1) * 128, :], comb[:, m, :])
                Y(y_out)
            barrier()

        def y_fin(y):
            y.wait_ge(D["out"].h, D["out"].n)
            y.wait_ge(D["sst"].h, D["sst"].n)
            y.wait_ge(D["xn"].h, D["xn"].n)
            y.wait_ge(D["pst"].h, D["pst"].n)
        Y(y_fin)

    return nc


def kernel(L0_attn_x, L0_attn_Wq, L0_attn_Wk, L0_attn_Wv, L0_attn_Wo,
           L0_attn_rms_w, L0_ffn_w_up, L0_ffn_w_down):
    x = np.asarray(L0_attn_x, dtype=np.float32)
    wq = np.ascontiguousarray(np.asarray(L0_attn_Wq, dtype=np.float32))
    wk = np.ascontiguousarray(np.asarray(L0_attn_Wk, dtype=np.float32))
    wv = np.ascontiguousarray(np.asarray(L0_attn_Wv, dtype=np.float32))
    wo = np.ascontiguousarray(np.asarray(L0_attn_Wo, dtype=np.float32))
    rms = np.asarray(L0_attn_rms_w, dtype=np.float32)
    wupT = np.ascontiguousarray(np.asarray(L0_ffn_w_up, dtype=np.float32).T)
    wdnT = np.ascontiguousarray(np.asarray(L0_ffn_w_down, dtype=np.float32).T)
    rmsw = np.ascontiguousarray(rms.reshape(NH, 128).T)

    nc = build_nc()
    in_maps = []
    for core in range(8):
        b, half = core // 2, core % 2
        consts = _host_consts(half)
        xm = x[b, half * 1024:(half + 1) * 1024, :]
        xp = x[b, 0:1024, :]
        in_maps.append(dict(
            xT=np.ascontiguousarray(xm.T), xpT=np.ascontiguousarray(xp.T),
            xtok=np.ascontiguousarray(xm),
            wq=wq, wk=wk, wv=wv, wo=wo, wupT=wupT, wdnT=wdnT, rmsw=rmsw,
            **consts,
        ))
    res = run_bass_kernel_spmd(nc, in_maps, core_ids=list(range(8)))
    out = np.empty((4, 2048, E), dtype=np.float32)
    for core in range(8):
        b, half = core // 2, core % 2
        out[b, half * 1024:(half + 1) * 1024, :] = res.results[core]["out"]
    return out


# version 10 (final)
# speedup vs baseline: 1.0591x; 1.0591x over previous
import math
import numpy as np
from contextlib import ExitStack

import concourse.bass as bass
import concourse.mybir as mybir
from concourse.bass_utils import run_bass_kernel_spmd

F32 = mybir.dt.float32
BF16 = mybir.dt.bfloat16
AF = mybir.ActivationFunctionType
ALU = mybir.AluOpType
AX = mybir.AxisListType

MAGIC = float(3 << 22)
E = 2048
T_CORE = 1024
SC = 512
NSC = 2
NT = 4
NH = 16
NKT = 16
I_FFN = 4096
NOT = 32
RMS_EPS = 1e-5
LN_EPS = 1e-5
INV_SQ1 = 1.0 / math.sqrt(E)
INV_SQ2 = 1.0 / math.sqrt(I_FFN)



def _slopes(n=16):
    start = 2.0 ** (-2.0 ** (-(math.log2(n) - 3)))
    return np.array([start * start ** i for i in range(n)], dtype=np.float64)


def _rope_tables(positions):
    freq = (1.0 / 1024.0) ** np.linspace(0.0, 1.0, 32, dtype=np.float64)
    theta = positions[:, None].astype(np.float64) * freq[None, :]
    ntile = len(positions) // 128
    cos = np.cos(theta).reshape(ntile, 128, 32).transpose(1, 0, 2)
    sin = np.sin(theta).reshape(ntile, 128, 32).transpose(1, 0, 2)
    return (np.ascontiguousarray(cos.astype(np.float32)),
            np.ascontiguousarray(sin.astype(np.float32)))


def _hadamard(n):
    h = np.array([[1.0]])
    while h.shape[0] < n:
        h = np.block([[h, h], [h, -h]])
    return h.astype(np.float32)


def _host_bfly_perm(ntiles):
    rng = np.random.default_rng(0)
    x = rng.standard_normal((ntiles, 9))

    def fwht_ref(a):
        a = a.copy()
        h = 1
        while h < a.shape[0]:
            for i in range(0, a.shape[0], 2 * h):
                for k in range(i, i + h):
                    u, w = a[k].copy(), a[k + h].copy()
                    a[k], a[k + h] = u + w, u - w
            h *= 2
        return a

    y = fwht_ref(x)
    a = x.copy()
    s = ntiles // 2
    while s >= 1:
        for grp in range(ntiles // (2 * s)):
            base = grp * 2 * s
            for i in range(s):
                lo, hi = a[base + i].copy(), a[base + s + i].copy()
                a[base + i] = lo - hi
                a[base + s + i] = lo + hi
        s //= 2
    perm = np.zeros(ntiles, dtype=int)
    for slot in range(ntiles):
        perm[slot] = int(np.where(np.all(np.isclose(y, a[slot]), axis=1))[0][0])
    return perm


PERM16 = _host_bfly_perm(16)
PERM32 = _host_bfly_perm(32)


def _host_consts(half):
    sl = _slopes(NH)
    r = np.exp(-sl)
    c = np.arange(128, dtype=np.float64)
    diff = c[None, :] - c[:, None]
    dm = np.where(diff >= 0, r[:, None, None] ** np.where(diff >= 0, diff, 0.0)[None], 0.0)
    dmaskT = np.ascontiguousarray(dm.transpose(1, 0, 2).reshape(128, NH * 128).astype(np.float32))
    qdec = (r[:, None] ** (c[None, :] + 1.0)).reshape(1, NH * 128).astype(np.float32)
    kdec = (r[None, :] ** (127.0 - c[:, None])).astype(np.float32)
    rC = np.ascontiguousarray(np.broadcast_to((r ** 128.0)[None, :], (128, NH)).astype(np.float32))
    mask = np.full((128, 1), 1.0 if half == 1 else 0.0, dtype=np.float32)
    ropec, ropes = _rope_tables(np.arange(half * 1024, half * 1024 + 1024))
    ropecp, ropesp = _rope_tables(np.arange(0, 1024))
    return dict(dmaskT=dmaskT, qdec=np.ascontiguousarray(qdec), kdec=kdec, rCc=rC,
                maskc=mask, ropec=ropec, ropes=ropes, ropecp=ropecp, ropesp=ropesp,
                hmat=_hadamard(128), ident=np.eye(128, dtype=np.float32))


class Ctr:
    def __init__(self, handle):
        self.h = handle
        self.n = 0


def build_nc(dbg=False):
    nc = bass.Bass("TRN2", target_bir_lowering=False)

    d_xT = nc.dram_tensor("xT", [E, T_CORE], F32, kind="ExternalInput").ap()
    d_xpT = nc.dram_tensor("xpT", [E, T_CORE], F32, kind="ExternalInput").ap()
    d_xtok = nc.dram_tensor("xtok", [T_CORE, E], F32, kind="ExternalInput").ap()
    d_wq = nc.dram_tensor("wq", [E, E], F32, kind="ExternalInput").ap()
    d_wk = nc.dram_tensor("wk", [E, E], F32, kind="ExternalInput").ap()
    d_wv = nc.dram_tensor("wv", [E, E], F32, kind="ExternalInput").ap()
    d_wo = nc.dram_tensor("wo", [E, E], F32, kind="ExternalInput").ap()
    d_wupT = nc.dram_tensor("wupT", [E, I_FFN], F32, kind="ExternalInput").ap()
    d_wdnT = nc.dram_tensor("wdnT", [I_FFN, E], F32, kind="ExternalInput").ap()
    d_rmsw = nc.dram_tensor("rmsw", [128, NH], F32, kind="ExternalInput").ap()
    d_ropec = nc.dram_tensor("ropec", [128, 8, 32], F32, kind="ExternalInput").ap()
    d_ropes = nc.dram_tensor("ropes", [128, 8, 32], F32, kind="ExternalInput").ap()
    d_ropecp = nc.dram_tensor("ropecp", [128, 8, 32], F32, kind="ExternalInput").ap()
    d_ropesp = nc.dram_tensor("ropesp", [128, 8, 32], F32, kind="ExternalInput").ap()
    d_hmat = nc.dram_tensor("hmat", [128, 128], F32, kind="ExternalInput").ap()
    d_ident = nc.dram_tensor("ident", [128, 128], F32, kind="ExternalInput").ap()
    d_dmaskT = nc.dram_tensor("dmaskT", [128, NH * 128], F32, kind="ExternalInput").ap()
    d_qdec = nc.dram_tensor("qdec", [1, NH * 128], F32, kind="ExternalInput").ap()
    d_kdec = nc.dram_tensor("kdec", [128, NH], F32, kind="ExternalInput").ap()
    d_rCc = nc.dram_tensor("rCc", [128, NH], F32, kind="ExternalInput").ap()
    d_maskc = nc.dram_tensor("maskc", [128, 1], F32, kind="ExternalInput").ap()
    d_out = nc.dram_tensor("out", [T_CORE, E], F32, kind="ExternalOutput").ap()

    s_wq = nc.dram_tensor("s_wq", [E, E], BF16, kind="Internal")
    s_wk = nc.dram_tensor("s_wk", [E, E], BF16, kind="Internal")
    s_wv = nc.dram_tensor("s_wv", [E, E], BF16, kind="Internal")
    s_wo = nc.dram_tensor("s_wo", [E, E], BF16, kind="Internal")
    s_tup = nc.dram_tensor("s_tup", [E, I_FFN], BF16, kind="Internal")
    s_tdn = nc.dram_tensor("s_tdn", [I_FFN, E], BF16, kind="Internal")
    okind = "ExternalOutput" if dbg else "Internal"
    s_s = nc.dram_tensor("s_s", [T_CORE, E], F32, kind=okind)
    s_xnT = nc.dram_tensor("s_xnT", [E, T_CORE], BF16, kind=okind)
    if dbg:
        d_attnw = nc.dram_tensor("d_attnw", [E, T_CORE], BF16, kind="ExternalOutput").ap()
        d_qT = nc.dram_tensor("d_qT", [E, T_CORE], BF16, kind="ExternalOutput").ap()
        d_kT = nc.dram_tensor("d_kT", [E, T_CORE], BF16, kind="ExternalOutput").ap()
        d_v = nc.dram_tensor("d_v", [T_CORE, E], BF16, kind="ExternalOutput").ap()
        d_kv = nc.dram_tensor("d_kv", [128, E], F32, kind="ExternalOutput").ap()
        d_int1 = nc.dram_tensor("d_int1", [E, T_CORE], BF16, kind="ExternalOutput").ap()
        d_int2 = nc.dram_tensor("d_int2", [I_FFN, T_CORE], BF16, kind="ExternalOutput").ap()
        d_g = nc.dram_tensor("d_g", [T_CORE, E], F32, kind="ExternalOutput").ap()

    with ExitStack() as ctx:
        en = ctx.enter_context

        c_rope = en(nc.sbuf_tensor("c_rope", [128, 4, 8, 32], F32))
        c_hmat = en(nc.sbuf_tensor("c_hmat", [128, 128], BF16))
        c_idf = en(nc.sbuf_tensor("c_idf", [128, 128], F32))
        c_idb = en(nc.sbuf_tensor("c_idb", [128, 128], BF16))
        c_dmaskT = en(nc.sbuf_tensor("c_dmaskT", [128, NH, 128], BF16))
        c_qdec = en(nc.sbuf_tensor("c_qdec", [128, NH, 128], BF16))
        c_kdec = en(nc.sbuf_tensor("c_kdec", [128, NH], F32))
        c_rC = en(nc.sbuf_tensor("c_rC", [128, NH], F32))
        c_rmsw = en(nc.sbuf_tensor("c_rmsw", [128, NH], F32))
        c_mask = en(nc.sbuf_tensor("c_mask", [128, 1], F32))
        c_ones_c = en(nc.sbuf_tensor("c_ones_c", [128, 1], F32))
        c_ones_r = en(nc.sbuf_tensor("c_ones_r", [1, 128], F32))
        c_one11 = en(nc.sbuf_tensor("c_one11", [1, 1], F32))
        c_mg = en(nc.sbuf_tensor("c_mg", [128, 1], F32))
        c_mgn = en(nc.sbuf_tensor("c_mgn", [128, 1], F32))
        c_zero = en(nc.sbuf_tensor("c_zero", [128, 1], F32))

        kv_f = en(nc.sbuf_tensor("kv_f", [128, NH, 128], F32))
        kv_b = en(nc.sbuf_tensor("kv_b", [128, NH, 128], BF16))

        bigA = en(nc.sbuf_tensor("bigA", [128, 16384], BF16))
        bigB = en(nc.sbuf_tensor("bigB", [128, 16384], BF16))
        bigC = en(nc.sbuf_tensor("bigC", [128, 8192], BF16))
        bigD = en(nc.sbuf_tensor("bigD", [128, 8192], F32))
        xnT = en(nc.sbuf_tensor("xnT", [128, NKT, SC], BF16))
        wbuf = en(nc.sbuf_tensor("wbuf", [128, 16384], BF16))

        xtile = en(nc.sbuf_tensor("xtile", [128, 2, 512], F32))
        sqt = en(nc.sbuf_tensor("sqt", [128, 2, 512], F32))
        rowbuf = en(nc.sbuf_tensor("rowbuf", [1, 4, 512], F32))
        scosb = en(nc.sbuf_tensor("scosb", [128, 2, 128], BF16))
        qdecb = en(nc.sbuf_tensor("qdecb", [128, 2, 128], BF16))
        sqacc = en(nc.sbuf_tensor("sqacc", [128, NT, 128], F32))
        rscol = en(nc.sbuf_tensor("rscol", [128, NT], F32))
        srow = en(nc.sbuf_tensor("srow", [1, 4, 128], F32))
        acol = en(nc.sbuf_tensor("acol", [128, 16], F32))
        gcol = en(nc.sbuf_tensor("gcol", [128, NT], F32))
        wsc = en(nc.sbuf_tensor("wsc", [128, 6], F32))
        wred = en(nc.sbuf_tensor("wred", [128, 4], F32))
        wred64 = en(nc.sbuf_tensor("wred64", [128, 64], F32))

        psf = [en(nc.psum_tensor(f"psf{i}", [128, 512], F32)) for i in range(6)]
        psb = [en(nc.psum_tensor(f"psb{i}", [128, 512], BF16)) for i in range(2)]

        sem_t = Ctr(en(nc.semaphore("sem_t")))
        sem_v = Ctr(en(nc.semaphore("sem_v")))
        sem_s = Ctr(en(nc.semaphore("sem_s")))
        sem_g = Ctr(en(nc.semaphore("sem_g")))
        dname = ["c", "gc", "pa0", "pa1", "pa2", "pa3", "pq0", "pq1", "pq2",
                 "pq3", "pst", "lnA", "lnB",
                 "wsA", "wsB", "xn", "sst", "up", "dn", "cx", "out", "dbg"]
        D = {n: Ctr(en(nc.semaphore(f"sem_{n}"))) for n in dname}
        ALLS = [sem_t, sem_v, sem_s, sem_g] + [D[n] for n in dname]
        sem_wc = Ctr(en(nc.semaphore("sem_wc")))

        def dma(eng, sem, out, in_):
            eng.dma_start(out=out, in_=in_).then_inc(sem.h, 16)
            sem.n += 16
            return sem.n

        block = en(nc.Block())

        def T(fn):
            @block.tensor
            def _(t):
                fn(t)

        def V(fn):
            @block.vector
            def _(v):
                fn(v)

        def S(fn):
            @block.scalar
            def _(s):
                fn(s)

        def G(fn):
            @block.gpsimd
            def _(g):
                fn(g)

        def Y(fn):
            @block.sync
            def _(y):
                fn(y)

        def barrier():
            snap = [(c, c.n) for c in ALLS if c.n > 0]
            for opener in (T, V, S, G, Y):
                def f(eng, snap=snap):
                    for cc, n in snap:
                        eng.wait_ge(cc.h, n)
                opener(f)

        wsl = wbuf[:].rearrange("p (a k c) -> p a k c", a=2, k=16)
        dnsl = wbuf[:].rearrange("p (k c) -> p k c", k=32)
        upsl = wbuf[:].rearrange("p (a k c) -> p a k c", a=8, k=16)
        qkst = bigD[:].bitcast(BF16).rearrange("p (a e) -> p a e", a=8)
        comb = bigD[:].rearrange("p (a f) -> p a f", a=4)
        zb = bigD[:].rearrange("p (k c) -> p k c", k=16)
        zb2 = bigD[:].rearrange("p (k c) -> p k c", k=32)
        qT = bigA[:].rearrange("p (a k c) -> p a k c", a=2, k=16)
        vB = bigB[:].rearrange("p (a m e) -> p a m e", a=2, m=4)
        attnw = bigC[:].rearrange("p (k c) -> p k c", k=16)
        xq1 = bigC[:].rearrange("p (k c) -> p k c", k=16)
        h2T = bigA[:].rearrange("p (k c) -> p k c", k=32)
        xq2 = bigB[:].rearrange("p (k c) -> p k c", k=32)
        kAp = bigA[:].rearrange("p (m e) -> p m e", m=8)
        vAp = bigB[:].rearrange("p (m e) -> p m e", m=8)
        rowv = rowbuf[:].rearrange("a r f -> a (r f)")
        sqsb = bigD[:, 0:2048].rearrange("p (h c) -> p h c", c=128)
        attf = bigD[:, 2048:4096]
        wbufF = wbuf[:].bitcast(F32)
        m1b = wbufF[:, 0:4096].rearrange("p (k c) -> p k c", c=512)
        m1f = wbufF[:, 0:4096]
        m2f = wbufF[:, 4096:8192]
        kvff = kv_f[:].rearrange("p h c -> p (h c)")
        kvbf = kv_b[:].rearrange("p h c -> p (h c)")
        rot4 = [sqt[:, 0, :], sqt[:, 1, :], xtile[:, 0, :], xtile[:, 1, :]]

        def p0(y):
            for dst, src in ((c_rope[:, 0], d_ropec), (c_rope[:, 1], d_ropes),
                             (c_rope[:, 2], d_ropecp), (c_rope[:, 3], d_ropesp)):
                dma(y, D["c"], dst, src)
            dma(y, D["c"], c_idf[:], d_ident)
            dma(y, D["c"], c_kdec[:], d_kdec)
            dma(y, D["c"], c_rC[:], d_rCc)
            dma(y, D["c"], c_rmsw[:], d_rmsw)
            dma(y, D["c"], c_mask[:], d_maskc)
        Y(p0)

        def p0g(g):
            dma(g, D["gc"], c_hmat[:], d_hmat)
            dma(g, D["gc"], c_dmaskT[:].rearrange("p h c -> p (h c)"), d_dmaskT)
            dma(g, D["gc"], c_qdec[:].rearrange("p h c -> p (h c)"), d_qdec.partition_broadcast(128))
            dma(g, D["gc"], c_idb[:], d_ident)
            g.memset(c_ones_c[:], 1.0)
            g.memset(c_ones_r[:], 1.0)
            g.memset(c_one11[:], 1.0)
            g.memset(c_mg[:], MAGIC)
            g.memset(c_mgn[:], -MAGIC)
            g.memset(c_zero[:], 0.0)
            g.memset(kvff, 0.0)
            g.memset(kvbf, 0.0)
            g.engine_nop().then_inc(sem_g.h, 1); sem_g.n += 1
            for dsrc, ddst in ((d_wk, s_wk), (d_wv, s_wv)):
                dma(g, D["gc"], ddst.ap(), dsrc)
        G(p0g)

        wt4 = bigD[:].rearrange("p (a f) -> p a f", a=4)
        NCH = 64

        def chunk_src(i):
            if i < 32:
                ti, hf = i // 2, i % 2
                return d_wupT[ti * 128:(ti + 1) * 128, hf * 2048:(hf + 1) * 2048]
            j = i - 32
            return d_wdnT[j * 128:(j + 1) * 128, :]

        lt1 = []
        v1_base = sem_v.n

        def y_p1a(y):
            for i in range(NCH):
                if i >= 4:
                    y.wait_ge(sem_v.h, v1_base + (i - 3))
                sl = D["pa%d" % (i % 4)]
                dma(y, sl, wt4[:, i % 4, :], chunk_src(i))
                lt1.append((sl, sl.n))
        Y(y_p1a)

        def v_p1a0(v):
            for i in range(NCH):
                v.wait_ge(lt1[i][0].h, lt1[i][1])
                v.tensor_reduce(wred64[:, i:i + 1], wt4[:, i % 4, :], AX.X,
                                ALU.add, apply_absolute_value=True).then_inc(sem_v.h, 1)
                sem_v.n += 1
            v.tensor_reduce(wred[:, 0:1], wred64[:, 0:32], AX.X, ALU.add)
            v.tensor_reduce(wred[:, 1:2], wred64[:, 32:64], AX.X, ALU.add)
            v.drain()
            v.engine_nop().then_inc(sem_v.h, 1)
            sem_v.n += 1
        V(v_p1a0)
        tok_g = sem_v.n

        def t_p1a(t):
            t.wait_ge(sem_v.h, tok_g)
            t.matmul(psf[0][0:1, 0:2], c_ones_c[:], wred[:, 0:2], start=True, stop=True).then_inc(sem_t.h, 1)
            sem_t.n += 1
        T(t_p1a)
        tok_t1 = sem_t.n

        def v_p1a(v):
            v.wait_ge(sem_t.h, tok_t1)
            v.tensor_scalar(rowv[0:1, 0:1], psf[0][0:1, 0:1], 1.0 / (E * I_FFN), 1e-5, ALU.mult, ALU.max)
            v.tensor_scalar(rowv[0:1, 1:2], psf[0][0:1, 1:2], 1.0 / (E * I_FFN), 1e-5, ALU.mult, ALU.max)
            v.drain()
            v.reciprocal(rowv[0:1, 2:3], rowv[0:1, 0:1])
            v.reciprocal(rowv[0:1, 3:4], rowv[0:1, 1:2])
            v.drain()
            v.engine_nop().then_inc(sem_v.h, 1)
            sem_v.n += 1
        V(v_p1a)
        tok_v1 = sem_v.n

        def t_p1b(t):
            t.wait_ge(sem_v.h, tok_v1)
            t.matmul(psf[1][:, 0:4], c_ones_r[:], rowv[0:1, 0:4], start=True, stop=True).then_inc(sem_t.h, 1)
            sem_t.n += 1
        T(t_p1b)
        tok_t2 = sem_t.n

        def v_p1b(v):
            v.wait_ge(sem_t.h, tok_t2)
            v.tensor_copy(wsc[:, 0:4], psf[1][:, 0:4])
            v.drain()
            v.engine_nop().then_inc(sem_v.h, 1)
            sem_v.n += 1
        V(v_p1b)
        barrier()

        tb4 = bigB[:].bitcast(F32).rearrange("p (a f) -> p a f", a=4)
        q4 = bigA[:].rearrange("p (a f) -> p a f", a=8)
        lt2 = []
        s_qb = sem_s.n
        v_qb = sem_v.n
        pst_qb = D["pst"].n

        def y_p1q(y):
            for i in range(NCH):
                if i >= 4:
                    y.wait_ge(sem_s.h, s_qb + 2 * (i - 3))
                sl = D["pq%d" % (i % 4)]
                dma(y, sl, wt4[:, i % 4, :], chunk_src(i))
                lt2.append((sl, sl.n))
        Y(y_p1q)

        def s_p1q(s):
            for i in range(NCH):
                rc = wsc[:, 2:3] if i < 32 else wsc[:, 3:4]
                tgt = tb4[:, i % 4, :]
                s.wait_ge(lt2[i][0].h, lt2[i][1])
                if i >= 4:
                    s.wait_ge(sem_v.h, v_qb + (i - 3))
                s.activation(tgt, wt4[:, i % 4, :], AF.Identity, bias=c_mg[:], scale=rc).then_inc(sem_s.h, 1)
                sem_s.n += 1
                s.activation(tgt, tgt, AF.Identity, bias=c_mgn[:], scale=1.0).then_inc(sem_s.h, 1)
                sem_s.n += 1
        S(s_p1q)

        def v_p1q(v):
            for i in range(NCH):
                v.wait_ge(sem_s.h, s_qb + 2 * (i + 1))
                if i >= 4:
                    v.wait_ge(D["pst"].h, pst_qb + 16 * (i - 3))
                v.tensor_scalar(q4[:, i % 4, :], tb4[:, i % 4, :], 1.0, -1.0,
                                ALU.min, ALU.max).then_inc(sem_v.h, 1)
                sem_v.n += 1
        V(v_p1q)

        def g_p1q(g):
            for i in range(NCH):
                g.wait_ge(sem_v.h, v_qb + (i + 1))
                if i < 32:
                    ti, hf = i // 2, i % 2
                    dma(g, D["pst"], s_tup.ap()[ti * 128:(ti + 1) * 128, hf * 2048:(hf + 1) * 2048],
                        q4[:, i % 4, :])
                else:
                    j = i - 32
                    dma(g, D["pst"], s_tdn.ap()[j * 128:(j + 1) * 128, :], q4[:, i % 4, :])
        G(g_p1q)
        barrier()

        def g_wc(g):
            dma(g, sem_wc, s_wq.ap(), d_wq)
            dma(g, sem_wc, s_wo.ap(), d_wo)
        G(g_wc)

        def emit_lnb(xdram, sc0, waitextra=None):
            ltA = []
            s_base = sem_s.n
            t_base = sem_t.n

            def y_a(y):
                if waitextra is not None:
                    y.wait_ge(waitextra[0].h, waitextra[1])
                for k in range(NKT):
                    if k >= 2:
                        y.wait_ge(sem_s.h, s_base + (k - 1))
                        y.wait_ge(sem_t.h, t_base + 2 * (k - 1))
                    sl = D["lnA"] if k % 2 == 0 else D["lnB"]
                    dma(y, sl, xtile[:, k % 2, :],
                        xdram[k * 128:(k + 1) * 128, sc0 * SC:(sc0 + 1) * SC])
                    ltA.append((sl, sl.n))
            Y(y_a)

            def s_a(s):
                for k in range(NKT):
                    s.wait_ge(ltA[k][0].h, ltA[k][1])
                    if k >= 2:
                        s.wait_ge(sem_t.h, t_base + 2 * (k - 1))
                    s.activation(sqt[:, k % 2, :], xtile[:, k % 2, :], AF.Square).then_inc(sem_s.h, 1)
                    sem_s.n += 1
            S(s_a)

            def t_a(t):
                for k in range(NKT):
                    t.wait_ge(sem_s.h, s_base + k + 1)
                    t.matmul(psf[0][0:1, :], c_ones_c[:], xtile[:, k % 2, :],
                             start=(k == 0), stop=(k == NKT - 1))
                    t.matmul(psf[1][0:1, :], c_ones_c[:], sqt[:, k % 2, :],
                             start=(k == 0), stop=(k == NKT - 1)).then_inc(sem_t.h, 2)
                    sem_t.n += 2
            T(t_a)
            tok_sums = sem_t.n

            def v_a(v):
                v.wait_ge(sem_t.h, tok_sums)
                v.tensor_scalar(rowv[0:1, 0:512], psf[0][0:1, :], 1.0 / E, None, ALU.mult)
                v.tensor_scalar(rowv[0:1, 512:1024], psf[1][0:1, :], 1.0 / E, LN_EPS, ALU.mult, ALU.add)
                v.tensor_mul(rowv[0:1, 1024:1536], rowv[0:1, 0:512], rowv[0:1, 0:512])
                v.tensor_sub(rowv[0:1, 512:1024], rowv[0:1, 512:1024], rowv[0:1, 1024:1536]).then_inc(sem_v.h, 1)
                sem_v.n += 1
            V(v_a)
            tok_var = sem_v.n

            def s_b(s):
                s.wait_ge(sem_v.h, tok_var)
                s.activation(rowv[0:1, 1024:1536], rowv[0:1, 512:1024], AF.Sqrt).then_inc(sem_s.h, 1)
                sem_s.n += 1
            S(s_b)
            tok_sq = sem_s.n

            def v_b(v):
                v.wait_ge(sem_s.h, tok_sq)
                v.reciprocal(rowv[0:1, 512:1024], rowv[0:1, 1024:1536])
                v.tensor_mul(rowv[0:1, 0:512], rowv[0:1, 0:512], rowv[0:1, 512:1024]).then_inc(sem_v.h, 1)
                sem_v.n += 1
            V(v_b)
            tok_rows = sem_v.n

            def t_b(t):
                t.wait_ge(sem_v.h, tok_rows)
                t.matmul(psf[2][:], c_ones_r[:], rowv[0:1, 512:1024], start=True, stop=True)
                t.matmul(psf[3][:], c_ones_r[:], rowv[0:1, 0:512], start=True, stop=True).then_inc(sem_t.h, 1)
                sem_t.n += 1
            T(t_b)
            tok_bc = sem_t.n

            ltB = []
            v2_base = sem_v.n

            def y_b(y):
                for k in range(NKT):
                    if k < 2:
                        y.wait_ge(sem_t.h, tok_sums)
                    else:
                        y.wait_ge(sem_v.h, v2_base + 2 * (k - 1))
                    sl = D["lnA"] if k % 2 == 0 else D["lnB"]
                    dma(y, sl, xtile[:, k % 2, :],
                        xdram[k * 128:(k + 1) * 128, sc0 * SC:(sc0 + 1) * SC])
                    ltB.append((sl, sl.n))
            Y(y_b)

            def v_c(v):
                v.wait_ge(sem_t.h, tok_bc)
                for k in range(NKT):
                    v.wait_ge(ltB[k][0].h, ltB[k][1])
                    v.tensor_mul(sqt[:, k % 2, :], xtile[:, k % 2, :], psf[2][:]).then_inc(sem_v.h, 1)
                    sem_v.n += 1
                    v.tensor_sub(xnT[:, k, :], sqt[:, k % 2, :], psf[3][:]).then_inc(sem_v.h, 1)
                    sem_v.n += 1
            V(v_c)
            return (sem_v, sem_v.n)

        def load_wslice(y, sem, dst, src_ap, col0, ncol):
            view = src_ap.rearrange("(k p) c -> k p c", p=128)
            return dma(y, sem, dst, view[:, :, col0:col0 + ncol].transpose([1, 0, 2]))

        def emit_proj(ws_list, tok_xn, evac):
            bank_free = [(sem_v, 0)] * 4
            slice_user = [(sem_t, 0), (sem_t, 0)]
            si = 0
            for widx, wsrc in enumerate(ws_list):
                for n in range(4):
                    buf = si % 2
                    su = slice_user[buf]

                    wsem = D["wsA"] if buf == 0 else D["wsB"]

                    def y_l(y, buf=buf, wsrc=wsrc, n=n, su=su, wsem=wsem):
                        y.wait_ge(su[0].h, su[1])
                        load_wslice(y, wsem, wsl[:, buf], wsrc.ap(), n * 512, 512)
                    Y(y_l)
                    tok_sl = (wsem, wsem.n)
                    for m in range(NT):
                        bank = m
                        bf = bank_free[bank]

                        def t_mm(t, buf=buf, m=m, bank=bank, tok_sl=tok_sl, bf=bf, tok_xn=tok_xn):
                            t.wait_ge(tok_sl[0].h, tok_sl[1])
                            t.wait_ge(tok_xn[0].h, tok_xn[1])
                            t.wait_ge(bf[0].h, bf[1])
                            for k in range(NKT):
                                ins = t.matmul(psf[bank][:], xnT[:, k, m * 128:(m + 1) * 128],
                                               wsl[:, buf, k, :], start=(k == 0), stop=(k == NKT - 1))
                            ins.then_inc(sem_t.h, 1)
                            sem_t.n += 1
                        T(t_mm)
                        bank_free[bank] = evac(widx, n, m, psf[bank], sem_t.n)
                    slice_user[buf] = (sem_t, sem_t.n)
                    si += 1
            return bank_free

        def emit_rotary(v, tile3, m_abs, rope0):
            cosb = c_rope[:, rope0, m_abs, :].unsqueeze(1).broadcast_to([128, NH, 32])
            sinb = c_rope[:, rope0 + 1, m_abs, :].unsqueeze(1).broadcast_to([128, NH, 32])
            x1 = tile3[:, :, 0:32]
            x2 = tile3[:, :, 64:96]
            tv = [r.rearrange("p (h k) -> p h k", k=32) for r in rot4]
            v.tensor_mul(tv[0], x1, cosb)
            v.tensor_mul(tv[1], x2, sinb)
            v.tensor_mul(tv[2], x2, cosb)
            v.tensor_mul(tv[3], x1, sinb)
            v.tensor_add(x1, tv[0], tv[1])
            v.tensor_sub(x2, tv[2], tv[3]).then_inc(sem_v.h, 1)
            sem_v.n += 1
            return (sem_v, sem_v.n)

        def emit_kv_chunk(ksd_t, v_t, tok_ready, inter_tok, do_cast=True):
            vg = []
            vg_base = sem_v.n
            for gi in range(4):
                bank = psf[4 + gi % 2]

                def t_kv(t, gi=gi, bank=bank, tr=tok_ready):
                    t.wait_ge(tr[0].h, tr[1])
                    if gi >= 2:
                        t.wait_ge(sem_v.h, vg_base + (gi - 1))
                    for hh in range(4):
                        h = gi * 4 + hh
                        ins = t.matmul(bank[:, hh * 128:(hh + 1) * 128],
                                       ksd_t[:, h * 128:(h + 1) * 128],
                                       v_t[:, h * 128:(h + 1) * 128], start=True, stop=True)
                    ins.then_inc(sem_t.h, 1)
                    sem_t.n += 1
                T(t_kv)
                tok_t = sem_t.n

                def v_kv(v, gi=gi, bank=bank, tok_t=tok_t, it=inter_tok):
                    v.wait_ge(sem_t.h, tok_t)
                    if it is not None:
                        v.wait_ge(it[0].h, it[1])
                    g0 = gi * 4
                    kvs = kv_f[:, g0:g0 + 4, :]
                    rcs = c_rC[:, g0:g0 + 4].unsqueeze(2).broadcast_to([128, 4, 128])
                    v.tensor_mul(kvs, kvs, rcs)
                    v.tensor_add(kvs, kvs, bank[:].rearrange("p (a c) -> p a c", a=4)).then_inc(sem_v.h, 1)
                    sem_v.n += 1
                V(v_kv)
                vg.append(sem_v.n)

            if not do_cast:
                return (sem_v, vg[-1])

            def g_cast(g, gt=vg[-1]):
                g.wait_ge(sem_v.h, gt)
                g.tensor_copy(kvbf, kvff).then_inc(sem_g.h, 1)
                sem_g.n += 1
            G(g_cast)
            return (sem_g, sem_g.n)

        kv_ready = (sem_g, 0)
        for sc0 in range(NSC):
            tok_xn = emit_lnb(d_xpT, sc0)
            barrier()
            ev_k = {}

            def ev_pre(widx, n, m, bank, tok_mm):
                def s_ev(s, widx=widx, n=n, m=m, bank=bank, tok_mm=tok_mm):
                    s.wait_ge(sem_t.h, tok_mm)
                    dst = kAp[:, m, n * 512:(n + 1) * 512] if widx == 0 else vAp[:, m, n * 512:(n + 1) * 512]
                    s.copy(dst, bank[:]).then_inc(sem_s.h, 1)
                    sem_s.n += 1
                S(s_ev)
                ev_k[(widx, m, n)] = sem_s.n
                return (sem_s, sem_s.n)

            emit_proj([s_wk, s_wv], tok_xn, ev_pre)
            barrier()

            for m in range(NT):
                def v_rot(v, m=m, sc0=sc0):
                    k3 = kAp[:, m, :].rearrange("p (h d) -> p h d", d=128)
                    emit_rotary(v, k3, sc0 * NT + m, 2)
                    kd = c_kdec[:].unsqueeze(2).broadcast_to([128, NH, 128])
                    v.tensor_mul(vAp[:, 4 + m, :].rearrange("p (h d) -> p h d", d=128), k3, kd).then_inc(sem_v.h, 1)
                    sem_v.n += 1
                V(v_rot)
                kv_ready = emit_kv_chunk(vAp[:, 4 + m, :], vAp[:, m, :], (sem_v, sem_v.n),
                                         None, do_cast=False)
            barrier()

        def v_mask(v):
            v.tensor_scalar(kvff, kvff, c_mask[:], None, ALU.mult).then_inc(sem_v.h, 1)
            sem_v.n += 1
        V(v_mask)
        tok_m = sem_v.n

        def g_mask(g):
            g.wait_ge(sem_v.h, tok_m)
            g.tensor_copy(kvbf, kvff).then_inc(sem_g.h, 1)
            sem_g.n += 1
        G(g_mask)
        kv_ready = (sem_g, sem_g.n)
        if dbg:
            def y_dkv(y, kvt=kv_ready):
                y.wait_ge(kvt[0].h, kvt[1])
                dma(y, D["dbg"], d_kv, kvff)
            Y(y_dkv)
        barrier()

        for sc0 in range(NSC):
            tok_xn = emit_lnb(d_xT, sc0)
            barrier()

            def y_xnst(y, sc0=sc0):
                view = s_xnT.ap().rearrange("(k p) t -> k p t", p=128)
                dma(y, D["xn"], view[:, :, sc0 * SC:(sc0 + 1) * SC].transpose([1, 0, 2]), xnT[:])
            Y(y_xnst)

            rot_done = {}
            tp_done = {}

            def ev_main(widx, n, m, bank, tok_mm, sc0=sc0):
                if widx == 2:
                    def s_ev(s, n=n, m=m, bank=bank, tok_mm=tok_mm):
                        s.wait_ge(sem_t.h, tok_mm)
                        s.copy(vB[:, 0, m, n * 512:(n + 1) * 512], bank[:]).then_inc(sem_s.h, 1)
                        sem_s.n += 1
                    S(s_ev)
                    return (sem_s, sem_s.n)
                slot = widx * 4 + m

                def s_ev(s, slot=slot, n=n, bank=bank, tok_mm=tok_mm):
                    s.wait_ge(sem_t.h, tok_mm)
                    s.copy(qkst[:, slot, n * 512:(n + 1) * 512], bank[:]).then_inc(sem_s.h, 1)
                    sem_s.n += 1
                S(s_ev)
                ev = sem_s.n
                if n == 3:
                    def v_rot(v, widx=widx, m=m, slot=slot, ev=ev, sc0=sc0):
                        v.wait_ge(sem_s.h, ev)
                        t3 = qkst[:, slot, :].rearrange("p (h d) -> p h d", d=128)
                        emit_rotary(v, t3, sc0 * NT + m, 0)
                        if widx == 1:
                            kd = c_kdec[:].unsqueeze(2).broadcast_to([128, NH, 128])
                            v.tensor_mul(vB[:, 1, m, :].rearrange("p (h d) -> p h d", d=128),
                                         t3, kd).then_inc(sem_v.h, 1)
                            sem_v.n += 1
                    V(v_rot)
                    rot_done[(widx, m)] = sem_v.n

                    s_base = sem_s.n
                    prev_tp = tp_done.get("last", 0)

                    def t_tp(t, slot=slot, rd=sem_v.n, s_base=s_base, prev_tp=prev_tp):
                        t.wait_ge(sem_v.h, rd)
                        for tau in range(NKT):
                            if tau < 2:
                                if prev_tp:
                                    t.wait_ge(sem_s.h, prev_tp)
                            else:
                                t.wait_ge(sem_s.h, s_base + tau - 1)
                            ins = t.transpose(psb[tau % 2][:, 0:128],
                                              qkst[:, slot, tau * 128:(tau + 1) * 128], c_idb[:])
                            ins.then_inc(sem_t.h, 1)
                            sem_t.n += 1
                    T(t_tp)
                    t_first = sem_t.n - NKT

                    def s_tp(s, widx=widx, m=m, t_first=t_first):
                        for tau in range(NKT):
                            s.wait_ge(sem_t.h, t_first + tau + 1)
                            s.copy(qT[:, widx, tau, m * 128:(m + 1) * 128],
                                   psb[tau % 2][:, 0:128]).then_inc(sem_s.h, 1)
                            sem_s.n += 1
                    S(s_tp)
                    tp_done[(widx, m)] = sem_s.n
                    tp_done["last"] = sem_s.n
                return (sem_s, ev)

            if sc0 == 0:
                def y_wcw(y):
                    y.wait_ge(sem_wc.h, 32)
                Y(y_wcw)
            emit_proj([s_wq, s_wk, s_wv], tok_xn, ev_main)
            barrier()

            for j in range(NT):
                sc_ev = {}
                at_ev = {}
                at_tok = {}
                for h in range(NH):
                    sbank = psf[h % 2]
                    abank = psf[2 + h % 2]

                    def t_sc(t, h=h, j=j, sbank=sbank):
                        if (h - 2) in sc_ev:
                            t.wait_ge(sem_v.h, sc_ev[h - 2])
                        ins = t.matmul(sbank[:, 0:128], qT[:, 1, h, j * 128:(j + 1) * 128],
                                       qT[:, 0, h, j * 128:(j + 1) * 128], start=True, stop=True)
                        ins.then_inc(sem_t.h, 1)
                        sem_t.n += 1
                    T(t_sc)
                    tok_sc = sem_t.n

                    def v_sc(v, h=h, j=j, sbank=sbank, tok_sc=tok_sc):
                        v.wait_ge(sem_t.h, tok_sc)
                        if (h - 2) in at_tok:
                            v.wait_ge(sem_t.h, at_tok[h - 2])
                        v.tensor_mul(scosb[:, h % 2, :], sbank[:, 0:128], c_dmaskT[:, h, :]).then_inc(sem_v.h, 1)
                        sem_v.n += 1
                        v.tensor_mul(qdecb[:, h % 2, :], qT[:, 0, h, j * 128:(j + 1) * 128],
                                     c_qdec[:, h, :]).then_inc(sem_v.h, 1)
                        sem_v.n += 1
                    V(v_sc)
                    sc_ev[h] = sem_v.n - 1
                    tok_vsc = sem_v.n

                    def t_at(t, h=h, j=j, abank=abank, tok_vsc=tok_vsc, kvt=kv_ready):
                        t.wait_ge(sem_v.h, tok_vsc)
                        t.wait_ge(kvt[0].h, kvt[1])
                        if (h - 2) in at_ev:
                            t.wait_ge(sem_s.h, at_ev[h - 2])
                        t.matmul(abank[:, 0:128], vB[:, 0, j, h * 128:(h + 1) * 128],
                                 scosb[:, h % 2, :], start=True, stop=False)
                        ins = t.matmul(abank[:, 0:128], kv_b[:, h, :], qdecb[:, h % 2, :],
                                       start=False, stop=True)
                        ins.then_inc(sem_t.h, 1)
                        sem_t.n += 1
                    T(t_at)
                    at_tok[h] = sem_t.n
                    tok_at = sem_t.n

                    def s_at(s, h=h, j=j, abank=abank, tok_at=tok_at):
                        s.wait_ge(sem_t.h, tok_at)
                        s.activation(attnw[:, h, j * 128:(j + 1) * 128], abank[:, 0:128], AF.Identity,
                                     bias=c_zero[:], scale=c_rmsw[:, h:h + 1]).then_inc(sem_s.h, 1)
                        sem_s.n += 1
                    S(s_at)
                    at_ev[h] = sem_s.n

                    def v_sq(v, h=h, abank=abank, tok_at=tok_at):
                        v.wait_ge(sem_t.h, tok_at)
                        v.tensor_copy(sqsb[:, h, :], abank[:, 0:128])
                        v.tensor_mul(sqsb[:, h, :], sqsb[:, h, :], sqsb[:, h, :]).then_inc(sem_v.h, 1)
                        sem_v.n += 1
                    V(v_sq)

                def v_tree(v, j=j):
                    sq2 = bigD[:, 0:2048]
                    v.tensor_add(attf[:, 0:1024], sq2[:, 0:1024], sq2[:, 1024:2048])
                    v.tensor_add(attf[:, 1024:1536], attf[:, 0:512], attf[:, 512:1024])
                    v.tensor_add(attf[:, 1536:1792], attf[:, 1024:1280], attf[:, 1280:1536])
                    v.tensor_add(sqacc[:, j, :], attf[:, 1536:1664], attf[:, 1664:1792]).then_inc(sem_v.h, 1)
                    sem_v.n += 1
                V(v_tree)

                last = (sc0 == NSC - 1 and j == NT - 1)
                if not last:
                    kv_ready = emit_kv_chunk(vB[:, 1, j, :], vB[:, 0, j, :],
                                             (sem_v, sem_v.n), (sem_t, sem_t.n))
            barrier()

            for j in range(NT):
                def t_vr(t, j=j):
                    t.matmul(psf[0][0:1, 0:128], c_ones_c[:], sqacc[:, j, :], start=True, stop=True).then_inc(sem_t.h, 1)
                    sem_t.n += 1
                T(t_vr)
                tk1 = sem_t.n

                def v_vr(v, tk1=tk1):
                    v.wait_ge(sem_t.h, tk1)
                    v.tensor_scalar(srow[0:1, 0, :], psf[0][0:1, 0:128], 1.0 / E, RMS_EPS,
                                    ALU.mult, ALU.add).then_inc(sem_v.h, 1)
                    sem_v.n += 1
                V(v_vr)
                tk2 = sem_v.n

                def s_sx(s, tk2=tk2):
                    s.wait_ge(sem_v.h, tk2)
                    s.activation(srow[0:1, 1, :], srow[0:1, 0, :], AF.Sqrt).then_inc(sem_s.h, 1)
                    sem_s.n += 1
                S(s_sx)
                tk3 = sem_s.n

                def v_rc(v, tk3=tk3):
                    v.wait_ge(sem_s.h, tk3)
                    v.reciprocal(srow[0:1, 2, :], srow[0:1, 1, :]).then_inc(sem_v.h, 1)
                    sem_v.n += 1
                V(v_rc)
                tk4 = sem_v.n

                def t_tr(t, tk4=tk4):
                    t.wait_ge(sem_v.h, tk4)
                    t.matmul(psf[1][:, 0:1], srow[0:1, 2, :], c_one11[:], start=True, stop=True).then_inc(sem_t.h, 1)
                    sem_t.n += 1
                T(t_tr)
                tk5 = sem_t.n

                def v_cp(v, j=j, tk5=tk5):
                    v.wait_ge(sem_t.h, tk5)
                    v.tensor_copy(rscol[:, j:j + 1], psf[1][:, 0:1])
                    v.drain()
                    v.engine_nop().then_inc(sem_v.h, 1)
                    sem_v.n += 1
                V(v_cp)
            barrier()

            wo_bank_free = {2: (sem_v, 0), 3: (sem_v, 0)}
            wo_slice_user = [(sem_t, 0), (sem_t, 0)]
            for n in range(4):
                buf = n % 2
                su = wo_slice_user[buf]

                wsem = D["wsA"] if buf == 0 else D["wsB"]

                def y_wo(y, buf=buf, n=n, su=su, wsem=wsem):
                    y.wait_ge(su[0].h, su[1])
                    load_wslice(y, wsem, wsl[:, buf], s_wo.ap(), n * 512, 512)
                Y(y_wo)
                tok_sl = (wsem, wsem.n)
                for m in range(NT):
                    bank = 2 + m % 2
                    bf = wo_bank_free[bank]

                    def t_wo(t, buf=buf, m=m, bank=bank, tok_sl=tok_sl, bf=bf):
                        t.wait_ge(tok_sl[0].h, tok_sl[1])
                        t.wait_ge(bf[0].h, bf[1])
                        for k in range(NH):
                            ins = t.matmul(psf[bank][:], attnw[:, k, m * 128:(m + 1) * 128],
                                           wsl[:, buf, k, :], start=(k == 0), stop=(k == NH - 1))
                        ins.then_inc(sem_t.h, 1)
                        sem_t.n += 1
                    T(t_wo)
                    tokmm = sem_t.n

                    def v_ev(v, n=n, m=m, bank=bank, tokmm=tokmm):
                        v.wait_ge(sem_t.h, tokmm)
                        v.tensor_scalar(comb[:, m, n * 512:(n + 1) * 512], psf[bank][:],
                                        rscol[:, m:m + 1], None, ALU.mult).then_inc(sem_v.h, 1)
                        sem_v.n += 1
                    V(v_ev)
                    wo_bank_free[bank] = (sem_v, sem_v.n)
                    if n == 3:
                        def y_st(y, m=m, sc0=sc0, tokev=sem_v.n):
                            y.wait_ge(sem_v.h, tokev)
                            gm = sc0 * NT + m
                            dma(y, D["sst"], s_s.ap()[gm * 128:(gm + 1) * 128, :], comb[:, m, :])
                        Y(y_st)
                wo_slice_user[buf] = (sem_t, sem_t.n)
            if dbg:
                def y_dbg1(y, sc0=sc0):
                    qv = d_qT.rearrange("(k p) t -> k p t", p=128)
                    kvv = d_kT.rearrange("(k p) t -> k p t", p=128)
                    av = d_attnw.rearrange("(k p) t -> k p t", p=128)
                    dma(y, D["dbg"], qv[:, :, sc0 * SC:(sc0 + 1) * SC].transpose([1, 0, 2]), qT[:, 0])
                    dma(y, D["dbg"], kvv[:, :, sc0 * SC:(sc0 + 1) * SC].transpose([1, 0, 2]), qT[:, 1])
                    dma(y, D["dbg"], av[:, :, sc0 * SC:(sc0 + 1) * SC].transpose([1, 0, 2]), attnw[:])
                    for m in range(NT):
                        gm = sc0 * NT + m
                        dma(y, D["dbg"], d_v[gm * 128:(gm + 1) * 128, :], vB[:, 0, m, :])
                Y(y_dbg1)
            barrier()

        for sc0 in range(NSC):
            def y_xn(y, sc0=sc0):
                view = s_xnT.ap().rearrange("(k p) t -> k p t", p=128)
                dma(y, D["xn"], xnT[:], view[:, :, sc0 * SC:(sc0 + 1) * SC].transpose([1, 0, 2]))
            Y(y_xn)
            tok_xn = (D["xn"], D["xn"].n)
            barrier()

            ev_base = sem_v.n
            for tau in range(NKT):
                bank = psf[tau % 2]

                def t_f1(t, tau=tau, bank=bank, tok_xn=tok_xn, ev_base=ev_base):
                    t.wait_ge(tok_xn[0].h, tok_xn[1])
                    if tau >= 2:
                        t.wait_ge(sem_v.h, ev_base + (tau - 1))
                    t.matmul(bank[:], c_hmat[:], xnT[:, tau, :], start=True, stop=True).then_inc(sem_t.h, 1)
                    sem_t.n += 1
                T(t_f1)
                tk = sem_t.n

                def v_f1(v, tau=tau, bank=bank, tk=tk):
                    v.wait_ge(sem_t.h, tk)
                    v.tensor_copy(zb[:, tau, :], bank[:]).then_inc(sem_v.h, 1)
                    sem_v.n += 1
                V(v_f1)
            tok = sem_v.n
            stride = NKT // 2
            while stride >= 1:
                def v_bf(v, stride=stride, tok=tok):
                    v.wait_ge(sem_v.h, tok)
                    ins = None
                    for grp in range(NKT // (2 * stride)):
                        base = grp * 2 * stride
                        lo = zb[:, base:base + stride, :].rearrange("p k c -> p (k c)")
                        hi = zb[:, base + stride:base + 2 * stride, :].rearrange("p k c -> p (k c)")
                        v.tensor_sub(lo, lo, hi)
                        ins = v.scalar_tensor_tensor(hi, hi, 2.0, lo, ALU.mult, ALU.add)
                    ins.then_inc(sem_v.h, 1)
                    sem_v.n += 1
                V(v_bf)
                tok = sem_v.n
                stride //= 2

            def v_q1t(v, tok=tok):
                v.wait_ge(sem_v.h, tok)
                cur = NKT
                first = True
                while cur > 1:
                    nh = cur // 2
                    w = nh * SC
                    if first:
                        a = zb[:, 0:nh, :].rearrange("p k c -> p (k c)")
                        b = zb[:, nh:cur, :].rearrange("p k c -> p (k c)")
                    else:
                        a, b = m1f[:, 0:w], m1f[:, w:2 * w]
                        a2, b2 = m2f[:, 0:w], m2f[:, w:2 * w]
                    v.tensor_tensor(m1f[:, 0:w], a, b, ALU.max)
                    if first:
                        v.tensor_tensor(m2f[:, 0:w], a, b, ALU.min)
                    else:
                        v.tensor_tensor(m2f[:, 0:w], a2, b2, ALU.min)
                    first = False
                    cur = nh
                v.engine_nop().then_inc(sem_v.h, 1)
                sem_v.n += 1
            V(v_q1t)
            tok_tree = sem_v.n

            for mt in range(NT):
                def t_m1(t, mt=mt, tok_tree=tok_tree):
                    t.wait_ge(sem_v.h, tok_tree)
                    if mt > 0:
                        t.wait_ge(sem_v.h, tok_tree + mt)
                    t.transpose(psf[4][:, 0:128], m1f[:, mt * 128:(mt + 1) * 128].unsqueeze(1).squeeze(1), c_idf[:])
                    t.transpose(psf[4][:, 128:256], m2f[:, mt * 128:(mt + 1) * 128].unsqueeze(1).squeeze(1),
                                c_idf[:]).then_inc(sem_t.h, 1)
                    sem_t.n += 1
                T(t_m1)
                tk = sem_t.n

                def v_m1(v, mt=mt, tk=tk):
                    v.wait_ge(sem_t.h, tk)
                    v.reduce_max(acol[:, mt:mt + 1], psf[4][:, 0:128], axis=AX.X)
                    v.tensor_reduce(acol[:, 14:15], psf[4][:, 128:256], AX.X, ALU.min)
                    v.drain()
                    v.tensor_scalar(acol[:, 14:15], acol[:, 14:15], -1.0, None, ALU.mult)
                    v.drain()
                    v.tensor_tensor(acol[:, mt:mt + 1], acol[:, mt:mt + 1], acol[:, 14:15], ALU.max)
                    v.drain()
                    v.engine_nop().then_inc(sem_v.h, 1)
                    sem_v.n += 1
                V(v_m1)
            tok_cols = sem_v.n

            def v_a1(v, tok_cols=tok_cols):
                v.wait_ge(sem_v.h, tok_cols)
                v.tensor_scalar(acol[:, 0:4], acol[:, 0:4], INV_SQ1, 1e-5, ALU.mult, ALU.max)
                v.drain()
                for mt in range(NT):
                    v.reciprocal(acol[:, 4 + mt:5 + mt], acol[:, mt:mt + 1])
                v.drain()
                v.tensor_scalar(acol[:, 4:8], acol[:, 4:8], 127.0 * INV_SQ1, None, ALU.mult)
                v.drain()
                v.engine_nop().then_inc(sem_v.h, 1)
                sem_v.n += 1
            V(v_a1)
            tok_a1 = sem_v.n

            def t_sr(t, tok_a1=tok_a1):
                t.wait_ge(sem_v.h, tok_a1)
                ins = None
                for mt in range(NT):
                    ins = t.matmul(psf[5][0:1, mt * 128:(mt + 1) * 128], acol[:, 4 + mt:5 + mt],
                                   c_idf[:], start=True, stop=True)
                ins.then_inc(sem_t.h, 1)
                sem_t.n += 1
            T(t_sr)
            tk_sr = sem_t.n

            def v_sr(v, tk_sr=tk_sr):
                v.wait_ge(sem_t.h, tk_sr)
                v.tensor_copy(rowv[0:1, 1536:2048], psf[5][0:1, :]).then_inc(sem_v.h, 1)
                sem_v.n += 1
            V(v_sr)
            tk_sr2 = sem_v.n

            def t_sb(t, tk_sr2=tk_sr2):
                t.wait_ge(sem_v.h, tk_sr2)
                t.matmul(psf[2][:], c_ones_r[:], rowv[0:1, 1536:2048], start=True, stop=True).then_inc(sem_t.h, 1)
                sem_t.n += 1
            T(t_sb)
            tk_sb = sem_t.n

            rq_base = sem_v.n
            for tau in range(NKT):
                def v_rq(v, tau=tau, tk_sb=tk_sb):
                    v.wait_ge(sem_t.h, tk_sb)
                    v.tensor_mul(sqt[:, 0, :], zb[:, tau, :], psf[2][:])
                    v.tensor_scalar(sqt[:, 0, :], sqt[:, 0, :], MAGIC, MAGIC, ALU.add, ALU.subtract)
                    v.tensor_scalar(xq1[:, tau, :], sqt[:, 0, :], 127.0, -127.0, ALU.min, ALU.max).then_inc(sem_v.h, 1)
                    sem_v.n += 1
                V(v_rq)
            tok_xq1 = sem_v.n
            if dbg:
                def y_dq1(y, sc0=sc0, tok_xq1=tok_xq1):
                    y.wait_ge(sem_v.h, tok_xq1)
                    dv = d_int1.rearrange("(k p) t -> k p t", p=128)
                    dma(y, D["dbg"], dv[:, :, sc0 * SC:(sc0 + 1) * SC].transpose([1, 0, 2]), xq1[:])
                Y(y_dq1)
            barrier()

            up_view = s_tup.ap().rearrange("(k p) o -> k p o", p=128)
            up_ev = {}
            up_tok = {}
            gsq_base = sem_g.n
            for og in range(4):
                def y_up(y, og=og, tok_xq1=tok_xq1):
                    if og == 0:
                        y.wait_ge(sem_v.h, tok_xq1)
                        y.wait_ge(D["pst"].h, D["pst"].n)
                    if og >= 1:
                        y.wait_ge(sem_t.h, up_tok[og - 1])
                    for slot in range(8):
                        om = og * 8 + slot
                        dma(y, D["up"], upsl[:, slot],
                            up_view[:, :, om * 128:(om + 1) * 128].transpose([1, 0, 2]))
                Y(y_up)
                tok_up = D["up"].n
                for slot in range(8):
                    om = og * 8 + slot
                    bank = psf[om % 4]

                    def t_up(t, slot=slot, om=om, bank=bank, tok_up=tok_up, tok_xq1=tok_xq1):
                        t.wait_ge(D["up"].h, tok_up)
                        t.wait_ge(sem_v.h, tok_xq1)
                        if om >= 4:
                            t.wait_ge(sem_s.h, up_ev[om - 4])
                        for k in range(NKT):
                            log = int(PERM16[k])
                            ins = t.matmul(bank[:], upsl[:, slot, log, :], xq1[:, k, :],
                                           start=(k == 0), stop=(k == NKT - 1))
                        ins.then_inc(sem_t.h, 1)
                        sem_t.n += 1
                    T(t_up)
                    tk = sem_t.n

                    def s_up(s, om=om, bank=bank, tk=tk, gsq_base=gsq_base):
                        s.wait_ge(sem_t.h, tk)
                        if om >= 2:
                            s.wait_ge(sem_g.h, gsq_base + (om - 1))
                        s.activation(sqt[:, om % 2, :], bank[:], AF.Relu).then_inc(sem_s.h, 1)
                        sem_s.n += 1
                    S(s_up)
                    up_ev[om] = sem_s.n

                    def g_sq(g, om=om, sev=sem_s.n):
                        g.wait_ge(sem_s.h, sev)
                        g.tensor_mul(h2T[:, om, :], sqt[:, om % 2, :],
                                     sqt[:, om % 2, :]).then_inc(sem_g.h, 1)
                        sem_g.n += 1
                    G(g_sq)
                up_tok[og] = sem_t.n
            barrier()

            for slab in range(2):
                c0 = slab * 256
                ev2 = sem_v.n
                for tau in range(NOT):
                    bank = psf[tau % 2]

                    def t_f2(t, tau=tau, bank=bank, ev2=ev2, c0=c0):
                        if tau >= 2:
                            t.wait_ge(sem_v.h, ev2 + (tau - 1))
                        t.matmul(bank[:, 0:256], c_hmat[:], h2T[:, tau, c0:c0 + 256],
                                 start=True, stop=True).then_inc(sem_t.h, 1)
                        sem_t.n += 1
                    T(t_f2)
                    tk = sem_t.n

                    def v_f2(v, tau=tau, bank=bank, tk=tk):
                        v.wait_ge(sem_t.h, tk)
                        v.tensor_copy(zb2[:, tau, 0:256], bank[:, 0:256]).then_inc(sem_v.h, 1)
                        sem_v.n += 1
                    V(v_f2)
                tok = sem_v.n
                stride = NOT // 2
                while stride >= 1:
                    def v_b2(v, stride=stride, tok=tok):
                        v.wait_ge(sem_v.h, tok)
                        ins = None
                        for grp in range(NOT // (2 * stride)):
                            base = grp * 2 * stride
                            lo = zb2[:, base:base + stride, 0:256].rearrange("p k c -> p (k c)")
                            hi = zb2[:, base + stride:base + 2 * stride, 0:256].rearrange("p k c -> p (k c)")
                            v.tensor_sub(lo, lo, hi)
                            ins = v.scalar_tensor_tensor(hi, hi, 2.0, lo, ALU.mult, ALU.add)
                        ins.then_inc(sem_v.h, 1)
                        sem_v.n += 1
                    V(v_b2)
                    tok = sem_v.n
                    stride //= 2

                m1v = m1f.rearrange("p (k c) -> p k c", c=256)

                def v_q2t(v, tok=tok):
                    v.wait_ge(sem_v.h, tok)
                    cur = NOT
                    first = True
                    while cur > 1:
                        nh = cur // 2
                        w = nh * 256
                        if first:
                            a = zb2[:, 0:nh, 0:256].rearrange("p k c -> p (k c)")
                            b = zb2[:, nh:cur, 0:256].rearrange("p k c -> p (k c)")
                            v.tensor_tensor(m1f[:, 0:w], a, b, ALU.max)
                            v.tensor_tensor(m2f[:, 0:w], a, b, ALU.min)
                        else:
                            v.tensor_tensor(m1f[:, 0:w], m1f[:, 0:w], m1f[:, w:2 * w], ALU.max)
                            v.tensor_tensor(m2f[:, 0:w], m2f[:, 0:w], m2f[:, w:2 * w], ALU.min)
                        first = False
                        cur = nh
                    v.engine_nop().then_inc(sem_v.h, 1)
                    sem_v.n += 1
                V(v_q2t)
                tok_tr2 = sem_v.n

                prev_vm2 = [0]
                for mt in range(2):
                    gm = slab * 2 + mt

                    def t_m2(t, mt=mt, tok_tr2=tok_tr2, pv=prev_vm2[0]):
                        t.wait_ge(sem_v.h, tok_tr2)
                        if pv:
                            t.wait_ge(sem_v.h, pv)
                        t.transpose(psf[4][:, 0:128], m1f[:, mt * 128:(mt + 1) * 128].unsqueeze(1).squeeze(1),
                                    c_idf[:])
                        t.transpose(psf[4][:, 128:256], m2f[:, mt * 128:(mt + 1) * 128].unsqueeze(1).squeeze(1),
                                    c_idf[:]).then_inc(sem_t.h, 1)
                        sem_t.n += 1
                    T(t_m2)
                    tk = sem_t.n

                    def v_m2(v, tk=tk):
                        v.wait_ge(sem_t.h, tk)
                        v.reduce_max(acol[:, 8:9], psf[4][:, 0:128], axis=AX.X)
                        v.tensor_reduce(acol[:, 14:15], psf[4][:, 128:256], AX.X, ALU.min)
                        v.tensor_scalar(acol[:, 14:15], acol[:, 14:15], -1.0, None, ALU.mult)
                        v.tensor_tensor(acol[:, 8:9], acol[:, 8:9], acol[:, 14:15],
                                        ALU.max).then_inc(sem_v.h, 1)
                        sem_v.n += 1
                    V(v_m2)
                    prev_vm2[0] = sem_v.n
                    tok_c2 = sem_v.n

                    def v_g2(v, gm=gm, tok_c2=tok_c2):
                        v.wait_ge(sem_v.h, tok_c2)
                        v.tensor_mul(acol[:, 10:11], acol[:, 0 + gm:1 + gm], wsc[:, 0:1])
                        v.drain()
                        v.tensor_scalar(acol[:, 10:11], acol[:, 10:11], 1.0 / 127.0, None, ALU.mult)
                        v.drain()
                        v.tensor_mul(acol[:, 10:11], acol[:, 10:11], acol[:, 10:11])
                        v.drain()
                        v.reciprocal(acol[:, 11:12], acol[:, 10:11])
                        v.drain()
                        v.tensor_scalar(acol[:, 11:12], acol[:, 11:12], 1e-5, None, ALU.mult)
                        v.tensor_scalar(acol[:, 8:9], acol[:, 8:9], INV_SQ2, None, ALU.mult)
                        v.drain()
                        v.tensor_tensor(acol[:, 8:9], acol[:, 8:9], acol[:, 11:12], ALU.max)
                        v.drain()
                        v.reciprocal(acol[:, 12:13], acol[:, 8:9])
                        v.drain()
                        v.tensor_scalar(acol[:, 12:13], acol[:, 12:13], 127.0 * INV_SQ2, None, ALU.mult)
                        v.tensor_mul(acol[:, 13:14], acol[:, 10:11], acol[:, 8:9])
                        v.drain()
                        v.tensor_mul(acol[:, 13:14], acol[:, 13:14], wsc[:, 1:2])
                        v.drain()
                        v.tensor_scalar(gcol[:, gm:gm + 1], acol[:, 13:14], 1.0 / 127.0, None, ALU.mult)
                        v.drain()
                        v.engine_nop().then_inc(sem_v.h, 1)
                        sem_v.n += 1
                    V(v_g2)
                    tok_g2 = sem_v.n

                    def t_r2(t, mt=mt, tok_g2=tok_g2):
                        t.wait_ge(sem_v.h, tok_g2)
                        t.matmul(psf[5][0:1, mt * 128:(mt + 1) * 128], acol[:, 12:13], c_idf[:],
                                 start=True, stop=True).then_inc(sem_t.h, 1)
                        sem_t.n += 1
                    T(t_r2)
                    tk2 = sem_t.n

                    def v_r2(v, mt=mt, tk2=tk2):
                        v.wait_ge(sem_t.h, tk2)
                        v.tensor_copy(srow[0:1, 2 + mt, :], psf[5][0:1, mt * 128:(mt + 1) * 128]).then_inc(sem_v.h, 1)
                        sem_v.n += 1
                    V(v_r2)
                tok_rows2 = sem_v.n

                def t_b2(t, tok_rows2=tok_rows2):
                    t.wait_ge(sem_v.h, tok_rows2)
                    t.matmul(psf[3][:, 0:256], c_ones_r[:],
                             srow[0:1, 2:4, :].rearrange("a r c -> a (r c)"),
                             start=True, stop=True).then_inc(sem_t.h, 1)
                    sem_t.n += 1
                T(t_b2)
                tk_b2 = sem_t.n

                for tau in range(NOT):
                    def v_rq2(v, tau=tau, c0=c0, tk_b2=tk_b2):
                        v.wait_ge(sem_t.h, tk_b2)
                        seg = zb2[:, tau, 0:256]
                        v.tensor_mul(sqt[:, 0, 0:256], seg, psf[3][:, 0:256])
                        v.tensor_scalar(sqt[:, 0, 0:256], sqt[:, 0, 0:256], MAGIC, MAGIC, ALU.add, ALU.subtract)
                        v.tensor_scalar(xq2[:, tau, c0:c0 + 256], sqt[:, 0, 0:256],
                                        127.0, -127.0, ALU.min, ALU.max).then_inc(sem_v.h, 1)
                        sem_v.n += 1
                    V(v_rq2)
            tok_xq2 = sem_v.n
            if dbg:
                def y_dq2(y, sc0=sc0, tok_xq2=tok_xq2):
                    y.wait_ge(sem_v.h, tok_xq2)
                    dv = d_int2.rearrange("(k p) t -> k p t", p=128)
                    dma(y, D["dbg"], dv[:, :, sc0 * SC:(sc0 + 1) * SC].transpose([1, 0, 2]), xq2[:])
                Y(y_dq2)
            barrier()

            dn_view = s_tdn.ap().rearrange("(k p) o -> k p o", p=128)
            dn_bank_free = {0: (sem_s, 0), 1: (sem_s, 0)}
            dn_tok = {}
            for n in range(4):
                def y_dn(y, n=n, tok_xq2=tok_xq2):
                    if n == 0:
                        y.wait_ge(sem_v.h, tok_xq2)
                        y.wait_ge(D["pst"].h, D["pst"].n)
                    if n >= 1:
                        y.wait_ge(sem_t.h, dn_tok[n - 1])
                    dma(y, D["dn"], dnsl[:], dn_view[:, :, n * 512:(n + 1) * 512].transpose([1, 0, 2]))
                Y(y_dn)
                tok_dn = D["dn"].n
                for m in range(NT):
                    bank = m % 2
                    bf = dn_bank_free[bank]

                    def t_dn(t, n=n, m=m, bank=bank, tok_dn=tok_dn, bf=bf, tok_xq2=tok_xq2):
                        t.wait_ge(D["dn"].h, tok_dn)
                        t.wait_ge(sem_v.h, tok_xq2)
                        t.wait_ge(bf[0].h, bf[1])
                        for k in range(NOT):
                            log = int(PERM32[k])
                            ins = t.matmul(psf[bank][:], xq2[:, k, m * 128:(m + 1) * 128],
                                           dnsl[:, log, :], start=(k == 0), stop=(k == NOT - 1))
                        ins.then_inc(sem_t.h, 1)
                        sem_t.n += 1
                    T(t_dn)
                    tk = sem_t.n

                    def s_dn(s, n=n, m=m, bank=bank, tk=tk):
                        s.wait_ge(sem_t.h, tk)
                        s.activation(comb[:, m, n * 512:(n + 1) * 512], psf[bank][:], AF.Identity,
                                     bias=c_zero[:], scale=gcol[:, m:m + 1]).then_inc(sem_s.h, 1)
                        sem_s.n += 1
                    S(s_dn)
                    dn_bank_free[bank] = (sem_s, sem_s.n)
                dn_tok[n] = sem_t.n

            cb_tok = {}
            for m in range(NT):
                gm = sc0 * NT + m

                def y_cx(y, gm=gm, m=m, dtk=None):
                    y.wait_ge(sem_t.h, dn_tok[3])
                    if (m - 1) in cb_tok:
                        y.wait_ge(sem_v.h, cb_tok[m - 1])
                    dma(y, D["cx"], m1b[:, 0:4, :].rearrange("p k c -> p (k c)"),
                        d_xtok[gm * 128:(gm + 1) * 128, :])
                    dma(y, D["cx"], m1b[:, 4:8, :].rearrange("p k c -> p (k c)"),
                        s_s.ap()[gm * 128:(gm + 1) * 128, :])
                Y(y_cx)
                tok_cx = D["cx"].n

                def s_si(s, m=m):
                    s.activation(comb[:, m, :], comb[:, m, :], AF.Silu).then_inc(sem_s.h, 1)
                    sem_s.n += 1
                S(s_si)
                tok_si = sem_s.n
                if dbg:
                    def y_dg(y, gm=gm, m=m, tok_si=tok_si):
                        y.wait_ge(sem_s.h, tok_si)
                        dma(y, D["dbg"], d_g[gm * 128:(gm + 1) * 128, :], comb[:, m, :])
                    Y(y_dg)
                    dbg_g_tok = D["dbg"].n

                def v_cb(v, m=m, tok_cx=tok_cx, tok_si=tok_si):
                    if dbg:
                        v.wait_ge(D["dbg"].h, dbg_g_tok)
                    v.wait_ge(D["cx"].h, tok_cx)
                    v.wait_ge(sem_s.h, tok_si)
                    sslice = m1b[:, 4:8, :].rearrange("p k c -> p (k c)")
                    xslice = m1b[:, 0:4, :].rearrange("p k c -> p (k c)")
                    v.tensor_mul(comb[:, m, :], comb[:, m, :], sslice)
                    v.tensor_add(comb[:, m, :], comb[:, m, :], xslice).then_inc(sem_v.h, 1)
                    sem_v.n += 1
                V(v_cb)
                tok_cb = sem_v.n
                cb_tok[m] = tok_cb

                def y_out(y, gm=gm, m=m, tok_cb=tok_cb):
                    y.wait_ge(sem_v.h, tok_cb)
                    dma(y, D["out"], d_out[gm * 128:(gm + 1) * 128, :], comb[:, m, :])
                Y(y_out)
            barrier()

        def y_fin(y):
            y.wait_ge(D["out"].h, D["out"].n)
            y.wait_ge(D["sst"].h, D["sst"].n)
            y.wait_ge(D["xn"].h, D["xn"].n)
            y.wait_ge(D["pst"].h, D["pst"].n)
        Y(y_fin)

    return nc


def kernel(L0_attn_x, L0_attn_Wq, L0_attn_Wk, L0_attn_Wv, L0_attn_Wo,
           L0_attn_rms_w, L0_ffn_w_up, L0_ffn_w_down):
    x = np.asarray(L0_attn_x, dtype=np.float32)
    wq = np.ascontiguousarray(np.asarray(L0_attn_Wq, dtype=np.float32))
    wk = np.ascontiguousarray(np.asarray(L0_attn_Wk, dtype=np.float32))
    wv = np.ascontiguousarray(np.asarray(L0_attn_Wv, dtype=np.float32))
    wo = np.ascontiguousarray(np.asarray(L0_attn_Wo, dtype=np.float32))
    rms = np.asarray(L0_attn_rms_w, dtype=np.float32)
    wupT = np.ascontiguousarray(np.asarray(L0_ffn_w_up, dtype=np.float32).T)
    wdnT = np.ascontiguousarray(np.asarray(L0_ffn_w_down, dtype=np.float32).T)
    rmsw = np.ascontiguousarray(rms.reshape(NH, 128).T)

    nc = build_nc()
    in_maps = []
    for core in range(8):
        b, half = core // 2, core % 2
        consts = _host_consts(half)
        xm = x[b, half * 1024:(half + 1) * 1024, :]
        xp = x[b, 0:1024, :]
        in_maps.append(dict(
            xT=np.ascontiguousarray(xm.T), xpT=np.ascontiguousarray(xp.T),
            xtok=np.ascontiguousarray(xm),
            wq=wq, wk=wk, wv=wv, wo=wo, wupT=wupT, wdnT=wdnT, rmsw=rmsw,
            **consts,
        ))
    res = run_bass_kernel_spmd(nc, in_maps, core_ids=list(range(8)))
    out = np.empty((4, 2048, E), dtype=np.float32)
    for core in range(8):
        b, half = core // 2, core % 2
        out[b, half * 1024:(half + 1) * 1024, :] = res.results[core]["out"]
    return out
